# version 41
# baseline (speedup 1.0000x reference)
import numpy as np
import ml_dtypes

import concourse.bass as bass
import concourse.mybir as mybir
import concourse.bass_utils as _bass_utils_mod
from concourse.bass_utils import run_bass_kernel_spmd

if not getattr(_bass_utils_mod, "_ant_ldwopt_patched", False):
    _orig_run_command = _bass_utils_mod.run_command

    def _run_command_ldwopt(cmd, **kw):
        cmd = ["--enable-ldw-opt=true" if c == "--enable-ldw-opt=false" else c
               for c in cmd]
        return _orig_run_command(cmd, **kw)

    _bass_utils_mod.run_command = _run_command_ldwopt
    _bass_utils_mod._ant_ldwopt_patched = True


F32 = mybir.dt.float32
BF16 = mybir.dt.bfloat16
I32 = mybir.dt.int32
I8 = mybir.dt.int8
AF = mybir.ActivationFunctionType
BF = ml_dtypes.bfloat16

D = 2048
NK = 16
T = 1024
NQKV = 18
NH = 16
INTER = 5504
NI = 43
EXP_SHIFT = np.float32(-8.0)
INV_SQRT_HD = np.float32(1.0 / np.sqrt(128.0))



class Plan:

    def __init__(self):
        self.steps = []
        self.n = {}

    def c(self, sem):
        return self.n.get(sem, 0)

    def add(self, eng, emit, waits=(), inc=None, amt=1):
        for s, v in waits:
            assert v <= self.c(s), (s, v, self.c(s))
        incs = []
        if inc is not None:
            self.n[inc] = self.c(inc) + amt
            incs.append((inc, amt))
        self.steps.append((eng, tuple(waits), emit, tuple(incs)))
        return self.c(inc) if inc else None


def build_nc(dbg=False):
    nc = bass.Bass("TRN2", target_bir_lowering=False)

    d_x = nc.dram_tensor("x_own", [128, NK * T], F32, kind="ExternalInput").ap()
    d_xo = nc.dram_tensor("x_oth", [128, NK * T], F32, kind="ExternalInput").ap()
    d_wq = nc.dram_tensor("wq", [NQKV, 128, NK * 128], F32, kind="ExternalInput").ap()
    d_wo = nc.dram_tensor("wo", [NK, 128, NK * 128], F32, kind="ExternalInput").ap()
    d_bq = nc.dram_tensor("bq", [128, NQKV], F32, kind="ExternalInput").ap()
    d_bo = nc.dram_tensor("bo", [128, NK], F32, kind="ExternalInput").ap()
    d_gs = nc.dram_tensor("gslice", [6, 128, NK * 128], I8, kind="ExternalInput").ap()
    d_us = nc.dram_tensor("uslice", [6, 128, NK * 128], I8, kind="ExternalInput").ap()
    d_ds = nc.dram_tensor("dslice", [2, 128, NI * 128], I8, kind="ExternalInput").ap()
    cc_g_in = nc.dram_tensor("cc_g_in", [128, 6 * 2048], BF16)
    cc_u_in = nc.dram_tensor("cc_u_in", [128, 6 * 2048], BF16)
    cc_d_in = nc.dram_tensor("cc_d_in", [128, 2 * 5504], BF16)
    cc_g_out = nc.dram_tensor("cc_g_out", [8 * 128, 6 * 2048], BF16)
    cc_u_out = nc.dram_tensor("cc_u_out", [8 * 128, 6 * 2048], BF16)
    cc_d_out = nc.dram_tensor("cc_d_out", [8 * 128, 2 * 5504], BF16)
    cc_b_in = nc.dram_tensor("cc_b_in", [128, 16], BF16)
    cc_b_out = nc.dram_tensor("cc_b_out", [8 * 128, 16], BF16)
    d_lut = nc.dram_tensor("lut", [1, 16], F32, kind="ExternalInput").ap()
    d_mask = nc.dram_tensor("mask", [128, 16 * T], BF16, kind="ExternalInput").ap()
    d_ident = nc.dram_tensor("ident", [128, 128], BF16, kind="ExternalInput").ap()
    d_ones = nc.dram_tensor("ones_b", [128, 1], BF16, kind="ExternalInput").ap()
    d_onesf = nc.dram_tensor("ones_f", [1, 128], F32, kind="ExternalInput").ap()
    d_onesbr = nc.dram_tensor("ones_br", [1, 128], BF16, kind="ExternalInput").ap()
    d_out = nc.dram_tensor("out", [128, NK * T], F32, kind="ExternalOutput").ap()
    d_dbg = {}
    if dbg:
        for name, cols in [("ln1", NK * T), ("kv", 4096), ("q0", T),
                           ("attout", NK * T), ("ln2", NK * T), ("xres1", NK * T),
                           ("ln3", NK * T), ("h0", T), ("ffn", NK * T),
                           ("q15a", T), ("q15b", T), ("e15", 8192),
                           ("vsm", 2048), ("vaug", 1040),
                           ("gw1", 2048), ("gw9", 2048), ("dw3", 5504),
                           ("gw0", 2048), ("uw0", 2048), ("h1", T),
                           ("ws0", 1024), ("ws1", 1024),
                           ("hlo", 25600), ("hhi", 18432)]:
            d_dbg[name] = nc.dram_tensor("dbg_" + name, [128, cols], BF16,
                                         kind="ExternalOutput").ap()

    P = Plan()

    from contextlib import ExitStack
    with ExitStack() as _ctx:
        A_X = _ctx.enter_context(nc.sbuf_tensor("A_X", [128, NK * T], BF16))
        A_LN = _ctx.enter_context(nc.sbuf_tensor("A_LN", [128, NK * T], BF16))
        A_QKV = _ctx.enter_context(nc.sbuf_tensor("A_QKV", [128, 20480], BF16))
        A_H = _ctx.enter_context(nc.sbuf_tensor("A_H", [128, 25600], BF16))
        A_WST = _ctx.enter_context(nc.sbuf_tensor("A_WST", [128, 4608], BF16))
        A_I8 = _ctx.enter_context(nc.sbuf_tensor("A_I8", [128, 2048], I8))
        IDXB = _ctx.enter_context(nc.sbuf_tensor("IDXB", [128, 2048], BF16))
        A_GA = _ctx.enter_context(nc.sbuf_tensor("A_GA", [128, 1024], BF16))
        A_GW = _ctx.enter_context(nc.sbuf_tensor("A_GW", [128, 2048], BF16))
        SILU = _ctx.enter_context(nc.sbuf_tensor("SILU", [128, 1024], BF16))
        SQT = _ctx.enter_context(nc.sbuf_tensor("SQT", [128, 512], BF16))
        OUTB = _ctx.enter_context(nc.sbuf_tensor("OUTB", [128, 1024], F32))
        STAT = _ctx.enter_context(nc.sbuf_tensor("STAT", [1, 3072], F32))
        VAUG = _ctx.enter_context(nc.sbuf_tensor("VAUG", [128, 16 * 65], BF16))
        LUTB = _ctx.enter_context(nc.sbuf_tensor("LUTB", [128, 16], F32))
        IDENT = _ctx.enter_context(nc.sbuf_tensor("IDENT", [128, 128], BF16))
        ONESB = _ctx.enter_context(nc.sbuf_tensor("ONESB", [128, 1], BF16))
        ONESF = _ctx.enter_context(nc.sbuf_tensor("ONESF", [1, 128], F32))
        ONESBR = _ctx.enter_context(nc.sbuf_tensor("ONESBR", [1, 128], BF16))
        BQ = _ctx.enter_context(nc.sbuf_tensor("BQ", [128, NQKV], F32))
        BO = _ctx.enter_context(nc.sbuf_tensor("BO", [128, NK], F32))
        LUT1 = _ctx.enter_context(nc.sbuf_tensor("LUT1", [1, 16], F32))
        RDB = _ctx.enter_context(nc.sbuf_tensor("RDB", [128, 512], F32))
        PSA = _ctx.enter_context(nc.psum_tensor("PSA", [128, 512], F32))
        PSB = _ctx.enter_context(nc.psum_tensor("PSB", [128, 512], F32))
        PSC = _ctx.enter_context(nc.psum_tensor("PSC", [128, 512], F32))
        PSD = _ctx.enter_context(nc.psum_tensor("PSD", [128, 512], F32))
        PSE = _ctx.enter_context(nc.psum_tensor("PSE", [128, 512], F32))
        PSF = _ctx.enter_context(nc.psum_tensor("PSF", [128, 512], F32))
        PSG = _ctx.enter_context(nc.psum_tensor("PSG", [128, 512], F32))
        PSH = _ctx.enter_context(nc.psum_tensor("PSH", [128, 512], F32))
        sem_L = _ctx.enter_context(nc.semaphore("L"))
        sem_G = _ctx.enter_context(nc.semaphore("G"))
        sem_T = _ctx.enter_context(nc.semaphore("T"))
        sem_V = _ctx.enter_context(nc.semaphore("V"))
        sem_S = _ctx.enter_context(nc.semaphore("S"))
        sem_C1 = _ctx.enter_context(nc.semaphore("C1"))
        sem_C2 = _ctx.enter_context(nc.semaphore("C2"))
        sem_C3 = _ctx.enter_context(nc.semaphore("C3"))
        sem_C4 = _ctx.enter_context(nc.semaphore("C4"))
        sem_C5 = _ctx.enter_context(nc.semaphore("C5"))
        sem_G2 = _ctx.enter_context(nc.semaphore("G2"))
        sem_xtra = {}
        for _sn in ("GE", "GO", "UE", "UO", "DE", "DO", "LE", "LO"):
            sem_xtra[_sn] = _ctx.enter_context(nc.semaphore(_sn))
        block = _ctx.enter_context(nc.Block())
        SEM = {"L": sem_L, "G": sem_G, "T": sem_T, "V": sem_V, "S": sem_S, "C1": sem_C1, "C2": sem_C2, "C3": sem_C3, "C4": sem_C4, "C5": sem_C5, "G2": sem_G2, **sem_xtra}

        def xres(k):
            return A_X[:, k * T:(k + 1) * T]

        def lns(k):
            return A_LN[:, k * T:(k + 1) * T]

        def qkv(ch):
            return A_QKV[:, ch * T:(ch + 1) * T]

        def kfm():
            return A_QKV[:, 16384:18432]

        def vfm():
            return A_QKV[:, 18432:20480]

        def hch(i):
            if i < 25:
                return A_H[:, i * T:(i + 1) * T]
            return A_QKV[:, (i - 25) * T:(i - 24) * T]

        E_OFF, MSK_OFF, VSM_OFF = 0, 8192, 16384
        def etile(sb):
            return A_H[:, E_OFF + sb * 512: E_OFF + (sb + 1) * 512]

        def mtile(sb):
            return A_H[:, MSK_OFF + sb * 512: MSK_OFF + (sb + 1) * 512]

        def vsm(sb):
            return A_H[:, VSM_OFF + sb * 128: VSM_OFF + (sb + 1) * 128]

        ln1o = A_H[:, 0:16384]
        xoth = A_QKV[:, 0:16384]

        s_mean = STAT[:, 0:1024]
        s_scr = STAT[:, 1024:2048]
        s_nmrs = STAT[:, 2048:3072]
        s_rstd = s_nmrs
        s_rd = STAT[:, 0:512]

        def dma(eng_name, dst, src, waits=(), sem="L"):
            P.add(eng_name, lambda e, d=dst, s=src: e.dma_start(out=d, in_=s),
                  waits=waits, inc=sem, amt=16)
            return P.c(sem)

        dma("sync", IDENT[:, :], d_ident)
        dma("sync", ONESB[:, :], d_ones)
        dma("sync", ONESF[:, :], d_onesf)
        dma("sync", ONESBR[:, :], d_onesbr)
        dma("sync", BQ[:, :], d_bq)
        dma("sync", BO[:, :], d_bo)
        dma("sync", LUT1[:, :], d_lut)
        consts_L = P.c("L")
        CW = ("L", consts_L)
        P.add("tensor", lambda e: e.matmul(
            PSA[:, 0:16], ONESF[:, :], LUT1[:, :], start=True, stop=True),
            waits=[("L", consts_L)], inc="T")
        P.add("scalar", lambda e: e.copy(LUTB[:, :], PSA[:, 0:16]),
              waits=[("T", P.c("T"))], inc="S")
        lutb_S = P.c("S")

        P.add("gpsimd", lambda e: e.dma_start(out=A_X[:, :], in_=d_x), inc="G", amt=16)
        x_G = P.c("G")

        def ln_pipeline(src_fn, dst_fn, nchunks, src_ready, per_kh=None,
                        chunk_done_cb=None):
            if per_kh is None:
                per_kh = {}
            """LayerNorm over feature-major chunks; stats via ones-matmuls.
            src_ready: list/fn of waits for chunk k being readable."""
            sq_S = [0] * nchunks
            for k in range(nchunks):
                w = list(src_ready(k))
                if k >= 1:
                    w.append(("T", P.c("T")))
                for half in range(2):
                    P.add("scalar", lambda e, k=k, half=half: e.activation(
                        SQT[:, :], src_fn(k)[:, half * 512:(half + 1) * 512],
                        AF.Square), waits=w if half == 0 else [("T", P.c("T"))],
                        inc="S")
                    sq_S[k] = P.c("S")
                    ps = (PSA, PSB)[half]
                    P.add("tensor", lambda e, k=k, half=half, ps=ps: e.matmul(
                        ps[0:1, :], ONESB[:, :],
                        src_fn(k)[:, half * 512:(half + 1) * 512],
                        start=(k == 0), stop=(k == nchunks - 1)),
                        waits=[("S", sq_S[k]), CW] if k == 0 and half == 0 else (),
                        inc="T")
                    ps2 = (PSC, PSD)[half]
                    P.add("tensor", lambda e, k=k, half=half, ps2=ps2: e.matmul(
                        ps2[0:1, :], ONESB[:, :], SQT[:, :],
                        start=(k == 0), stop=(k == nchunks - 1)),
                        waits=[("S", sq_S[k])], inc="T")
            mm_T = P.c("T")
            inv_n = 1.0 / 2048.0
            for half in range(2):
                psS = (PSA, PSB)[half]
                psQ = (PSC, PSD)[half]
                sl = slice(half * 512, (half + 1) * 512)
                P.add("vector", lambda e, psS=psS, sl=sl: e.tensor_copy(
                    s_mean[:, sl], psS[0:1, :]), waits=[("T", mm_T)], inc="V")
                P.add("vector", lambda e, psQ=psQ, sl=sl: e.tensor_copy(
                    s_scr[:, sl], psQ[0:1, :]), inc="V")
                P.add("vector", lambda e, sl=sl: e.scalar_tensor_tensor(
                    out=s_nmrs[:, sl], in0=s_mean[:, sl], scalar=inv_n,
                    in1=s_mean[:, sl], op0=mybir.AluOpType.mult,
                    op1=mybir.AluOpType.mult), inc="V")
                P.add("vector", lambda e, sl=sl: e.tensor_sub(
                    s_scr[:, sl], s_scr[:, sl], s_nmrs[:, sl]), inc="V")
                P.add("vector", lambda e, sl=sl: e.scalar_tensor_tensor(
                    out=s_mean[:, sl], in0=s_mean[:, sl], scalar=inv_n,
                    in1=s_mean[:, sl], op0=mybir.AluOpType.mult,
                    op1=mybir.AluOpType.bypass), inc="V")
            fix_V = P.c("V")
            P.add("scalar", lambda e: e.activation(
                s_rstd[:, :], s_scr[:, 0:1024], AF.Sqrt, bias=np.float32(1e-5),
                scale=np.float32(inv_n)), waits=[("V", fix_V)], inc="S")
            rstd_S = P.c("S")
            P.add("vector", lambda e: e.reciprocal(
                s_scr[:, 0:1024], s_rstd[:, :]), waits=[("S", rstd_S)], inc="V")
            P.add("vector", lambda e: e.scalar_tensor_tensor(
                out=s_nmrs[:, :], in0=s_mean[:, 0:1024], scalar=-1.0,
                in1=s_scr[:, 0:1024], op0=mybir.AluOpType.mult,
                op1=mybir.AluOpType.mult), inc="V")
            nmrs_V = P.c("V")
            bps = (PSE, PSF, PSG, PSH)
            srcs = (s_scr[:, 0:512], s_nmrs[:, 0:512],
                    s_scr[:, 512:1024], s_nmrs[:, 512:1024])
            for bi in range(4):
                P.add("tensor", lambda e, bi=bi: e.matmul(
                    bps[bi][:, :], ONESF[:, :], srcs[bi], start=True, stop=True),
                    waits=[("V", nmrs_V)] if bi == 0 else (), inc="T")
            bc_T = P.c("T")
            for k in range(nchunks):
                for half in range(2):
                    s2 = slice(half * 512, (half + 1) * 512)
                    pm = (PSE, PSG)[half]
                    pa = (PSF, PSH)[half]
                    P.add("vector", lambda e, k=k, s2=s2, pm=pm: e.tensor_mul(
                        dst_fn(k)[:, s2], src_fn(k)[:, s2], pm[:, :]),
                        waits=[("T", bc_T)] if k == 0 and half == 0 else (), inc="V")
                    P.add("vector", lambda e, k=k, s2=s2, pa=pa: e.tensor_add(
                        dst_fn(k)[:, s2], dst_fn(k)[:, s2], pa[:, :]), inc="V")
                    per_kh[(k, half)] = P.c("V")
                if chunk_done_cb is not None:
                    chunk_done_cb(k)
            return P.c("V")

        ln1_kh = {}
        ln1_V = ln_pipeline(xres, lns, NK, lambda k: [("G", x_G)], per_kh=ln1_kh)
        def dbg_store(name, src_ap, waits):
            if dbg:
                P.add("sync", lambda e, s=src_ap, d=d_dbg[name]: e.dma_start(
                    out=d, in_=s), waits=waits, inc="L", amt=16)
        dbg_store("ln1", A_LN[:, :], [("V", ln1_V)])

        P.add("gpsimd", lambda e: e.dma_start(out=xoth, in_=d_xo), inc="G", amt=16)
        xo_G = P.c("G")
        ln1o_kh = {}
        ln1o_V = ln_pipeline(lambda k: xoth[:, k * T:(k + 1) * T],
                             lambda k: ln1o[:, k * T:(k + 1) * T],
                             NK, lambda k: [("G", xo_G)], per_kh=ln1o_kh)

        gat_pieces = []
        for kind in ("g", "u"):
            for sl_ in range(6):
                for pc in range(2):
                    gat_pieces.append((kind, sl_, pc, 1024))
        for sl_ in range(2):
            for pc, w_ in enumerate([1024] * 5 + [384]):
                gat_pieces.append(("d", sl_, pc, w_))
        gat_state = {"i": 0, "nl": 0, "nr": 0, "cast_S": {}, "lut_V": {},
                     "store_L": {}, "load_L": {}}

        def issue_cc(inb, outb, wts, csem):
            P.add("gpsimd", lambda e, i=inb, o=outb: e.collective_compute(
                "AllGather", mybir.AluOpType.bypass,
                replica_groups=[list(range(8))],
                ins=[i.ap().opt()], outs=[o.ap().opt()]),
                waits=list(wts), inc=csem)

        def gp_load(gi):
            kind, sl_, pc, w_ = gat_pieces[gi]
            src = {"g": d_gs, "u": d_us, "d": d_ds}[kind]
            src_ap = src[sl_, :, :].rearrange("p (k o) -> p k o", o=128)[
                :, pc * 8:pc * 8 + w_ // 128, :]
            lw = [CW]
            if gi >= 2:
                lw.append(gat_state["cast_S"][gi - 2])
            b = A_I8[:, (gi % 2) * 1024:(gi % 2) * 1024 + w_]
            ls = ("LE", "LO")[gi % 2]
            P.add("sync", lambda e, b=b, s=src_ap: e.dma_start(
                out=b.rearrange("p (k o) -> p k o", o=128), in_=s),
                waits=lw, inc=ls, amt=16)
            gat_state["load_L"][gi] = (ls, P.c(ls))

        def gp_rest(gi):
            kind, sl_, pc, w_ = gat_pieces[gi]
            b = A_I8[:, (gi % 2) * 1024:(gi % 2) * 1024 + w_]
            ib = IDXB[:, (gi % 2) * 1024:(gi % 2) * 1024 + w_]
            cw = [gat_state["load_L"][gi]]
            if gi >= 2:
                cw.append(("V", gat_state["lut_V"][gi - 2]))
            P.add("vector", lambda e, ib=ib, b=b: e.tensor_copy(ib, b),
                  waits=cw, inc="V")
            gat_state["cast_S"][gi] = ("V", P.c("V"))
            gw = A_GW[:, (gi % 2) * 1024:(gi % 2) * 1024 + w_]
            pl = A_GA[:, 0:w_]
            vw = [gat_state["cast_S"][gi], ("S", lutb_S)]
            if gi >= 2:
                vw.append(gat_state["store_L"][gi - 2])
            P.add("vector", lambda e, ib=ib, gw=gw: e.tensor_scalar(
                gw, ib, 0.0, LUTB[:, 0:1],
                mybir.AluOpType.is_equal, mybir.AluOpType.mult),
                waits=vw, inc="V")
            for j in range(1, 16):
                P.add("vector", lambda e, ib=ib, pl=pl, j=j: e.tensor_scalar(
                    pl, ib, float(j), LUTB[:, j:j + 1],
                    mybir.AluOpType.is_equal, mybir.AluOpType.mult), inc="V")
                P.add("vector", lambda e, gw=gw, pl=pl: e.tensor_add(gw, gw, pl),
                      inc="V")
            gat_state["lut_V"][gi] = P.c("V")
            dstmap = {"g": (cc_g_in, 2048), "u": (cc_u_in, 2048),
                      "d": (cc_d_in, 5504)}
            dt_, wid = dstmap[kind]
            dst = dt_.ap()[:, sl_ * wid + pc * 1024: sl_ * wid + pc * 1024 + w_]
            ss = {"g": ("GE", "GO"), "u": ("UE", "UO"),
                  "d": ("DE", "DO")}[kind][gi % 2]
            P.add("sync", lambda e, dst=dst, gw=gw: e.dma_start(out=dst, in_=gw),
                  waits=[("V", gat_state["lut_V"][gi])], inc=ss, amt=16)
            gat_state["store_L"][gi] = (ss, P.c(ss))
            done = gi + 1
            if done == 12:
                gat_state["gW"] = [("GE", P.c("GE")), ("GO", P.c("GO"))]
            elif done == 24:
                gat_state["uW"] = [("UE", P.c("UE")), ("UO", P.c("UO"))]
            elif done == 36:
                gat_state["dW"] = [("DE", P.c("DE")), ("DO", P.c("DO"))]

        def gather_piece():
            if gat_state["nr"] >= len(gat_pieces):
                return
            while (gat_state["nl"] < len(gat_pieces)
                   and gat_state["nl"] <= gat_state["nr"] + 1):
                gp_load(gat_state["nl"])
                gat_state["nl"] += 1
            gp_rest(gat_state["nr"])
            gat_state["nr"] += 1
            gat_state["i"] = gat_state["nr"]

        def wslot_qkv(i):
            off = (i % 2) * 2304 + ((i // 2) % 2) * 128
            return A_WST[:, off:off + 2048]

        wload_G = {}
        wmm_T = {}
        evac_S = {}

        def stream_wchunk(i, src_ap, rhs_fn, nk, dst_evac, bias_ap, ps_pair,
                          extra_rhs=None, extra_ps=None, extra_dst=None,
                          rhs_ready_V=0, base_T=0, rhs_kh=None, par_sem=False):
            wbuf = wslot_qkv(i)
            w = []
            if i >= 2:
                w.append(("T", wmm_T[i - 2]))
            elif base_T:
                w.append(("T", base_T))
            gs = ("G2" if (par_sem and i % 2 == 1) else "G")
            P.add("gpsimd", lambda e, wbuf=wbuf, s=src_ap: e.dma_start(out=wbuf, in_=s),
                  waits=w, inc=gs, amt=16)
            wload_G[i] = (gs, P.c(gs))
            mm_waits = [wload_G[i]]
            if rhs_ready_V:
                mm_waits.append(("V", rhs_ready_V))
            if i >= 2 and (i - 2) in evac_S:
                mm_waits.append(("S", evac_S[i - 2]))
            elif i == 1 and extra_rhs is not None and 0 in evac_S:
                mm_waits.append(("S", evac_S[0]))
            first = True
            for k in range(nk):
                for half in range(2):
                    ps = ps_pair[half]
                    w_ = list(mm_waits) if first else []
                    if rhs_kh is not None and (k, half) in rhs_kh:
                        w_.append(("V", rhs_kh[(k, half)]))
                    P.add("tensor", lambda e, wbuf=wbuf, k=k, half=half, ps=ps: e.matmul(
                        ps[:, :], wbuf[:, k * 128:(k + 1) * 128],
                        rhs_fn(k)[:, half * 512:(half + 1) * 512],
                        start=(k == 0), stop=(k == nk - 1)),
                        waits=w_, inc="T")
                    first = False
                    if extra_rhs is not None:
                        ps2 = extra_ps[half]
                        P.add("tensor", lambda e, wbuf=wbuf, k=k, half=half, ps2=ps2: e.matmul(
                            ps2[:, :], wbuf[:, k * 128:(k + 1) * 128],
                            extra_rhs(k)[:, half * 512:(half + 1) * 512],
                            start=(k == 0), stop=(k == nk - 1)), inc="T")
            wmm_T[i] = P.c("T")
            for half in range(2):
                ps = ps_pair[half]
                P.add("scalar", lambda e, ps=ps, half=half: e.activation(
                    dst_evac[:, half * 512:(half + 1) * 512], ps[:, :],
                    AF.Identity, bias=bias_ap, scale=np.float32(1.0)),
                    waits=[("T", wmm_T[i]), CW], inc="S")
                if extra_rhs is not None:
                    ps2 = extra_ps[half]
                    P.add("scalar", lambda e, ps2=ps2, half=half: e.activation(
                        extra_dst[:, half * 512:(half + 1) * 512], ps2[:, :],
                        AF.Identity, bias=bias_ap, scale=np.float32(1.0)), inc="S")
            evac_S[i] = P.c("S")
            return evac_S[i]

        order = [16, 17] + list(range(16))
        for i, ch in enumerate(order):
            pair = ((PSA, PSB) if i % 2 == 0 else (PSC, PSD))
            if ch == 16:
                stream_wchunk(i, d_wq[ch, :, :], lns, NK, kfm()[:, 0:1024],
                              BQ[:, ch:ch + 1], pair,
                              extra_rhs=lambda k: ln1o[:, k * T:(k + 1) * T],
                              extra_ps=(PSE, PSF), extra_dst=kfm()[:, 1024:2048],
                              rhs_ready_V=max(ln1_V, ln1o_V), par_sem=True)
            elif ch == 17:
                stream_wchunk(i, d_wq[ch, :, :], lns, NK, vfm()[:, 0:1024],
                              BQ[:, ch:ch + 1], pair,
                              extra_rhs=lambda k: ln1o[:, k * T:(k + 1) * T],
                              extra_ps=(PSE, PSF), extra_dst=vfm()[:, 1024:2048],
                              rhs_ready_V=max(ln1_V, ln1o_V), par_sem=True)
            else:
                stream_wchunk(i, d_wq[ch, :, :], lns, NK, qkv(ch),
                              BQ[:, ch:ch + 1], pair, rhs_kh=ln1_kh, par_sem=True)
            gather_piece()
            if i == 1:
                kv_evac_S = evac_S[1]
                kv_mm_T = wmm_T[1]
                for sb in range(16):
                    P.add("sync", lambda e, sb=sb: e.dma_start_transpose(
                        A_H[:, VSM_OFF + sb * 128: VSM_OFF + (sb + 1) * 128],
                        vfm()[:, sb * 128:(sb + 1) * 128]),
                        waits=[("S", kv_evac_S)] if sb == 0 else (), inc="L", amt=16)
                vt_L = P.c("L")
                mask0_L = dma("sync", A_H[:, MSK_OFF:MSK_OFF + 8192].rearrange(
                    "p (s t) -> p s t", t=512),
                    d_mask.rearrange("p (s t) -> p s t", t=T)[:, :, 0:512],
                    waits=[("T", kv_mm_T)])
                P.add("vector", lambda e: e.memset(VAUG[:, :], 1.0), inc="V")
                P.add("vector", lambda e: e.tensor_copy(
                    VAUG[:, :].rearrange("p (s c) -> p s c", c=65)[:, :, 0:64],
                    A_H[:, VSM_OFF:VSM_OFF + 2048].rearrange(
                        "p (s c) -> p s c", c=128)[:, :, 0:64]),
                    waits=[("L", vt_L)], inc="V")
                vaug_V = P.c("V")
        qkv_done_S = P.c("S")
        qkv_last_T = P.c("T")
        issue_cc(cc_g_in, cc_g_out, gat_state["gW"], "C1")
        dbg_store("kv", A_QKV[:, 16384:20480], [("S", qkv_done_S)])
        dbg_store("q0", A_QKV[:, 0:T], [("S", qkv_done_S)])
        dbg_store("q15a", A_QKV[:, 15 * T:16 * T], [("S", qkv_done_S)])

        attout_ready_V = 0
        prev_av_T = 0
        exp_S_hist = []
        sc_rot = [0]
        attout_prev2 = {0: 0, 1: 0}
        pend = None
        for half in range(2):
            if half == 0:
                mask_L = mask0_L
            else:
                mask_L = dma("sync", A_H[:, MSK_OFF:MSK_OFF + 8192].rearrange(
                    "p (s t) -> p s t", t=512),
                    d_mask.rearrange("p (s t) -> p s t", t=T)[:, :, 512:1024],
                    waits=[("T", prev_av_T)])
            for head in range(NH):
                q_ap = A_QKV[:, head * T + half * 512: head * T + (half + 1) * 512]
                kbs = []
                for part in range(2):
                    for i in range(8):
                        tlo = max(i - 4 * half, 0) * 128
                        if tlo < 512:
                            kbs.append((part * 8 + i, tlo))
                kbs.sort(key=lambda x: x[1])
                avp = (PSC, PSD) if head % 2 == 0 else (PSF, PSG)
                for n, (sb, tlo) in enumerate(kbs):
                    ps = (PSA, PSB, PSH)[sc_rot[0] % 3]
                    w = []
                    if head == 0 and n == 0:
                        w = [("L", mask_L), ("S", qkv_done_S), ("V", vaug_V)]
                    if len(exp_S_hist) >= 3:
                        w.append(("S", exp_S_hist[-3]))
                    P.add("tensor", lambda e, sb=sb, tlo=tlo, ps=ps, q_ap=q_ap: e.matmul(
                        ps[:, tlo:512], kfm()[:, sb * 128:(sb + 1) * 128],
                        q_ap[:, tlo:512], start=True, stop=True), waits=w, inc="T")
                    sc_T = P.c("T")
                    P.add("scalar", lambda e, sb=sb, tlo=tlo, ps=ps: e.activation(
                        etile(sb)[:, tlo:512], ps[:, tlo:512], AF.Exp,
                        bias=EXP_SHIFT, scale=INV_SQRT_HD),
                        waits=[("T", sc_T)], inc="S")
                    exp_S_hist.append(P.c("S"))
                    sc_rot[0] += 1
                last_exp = exp_S_hist[-1]
                first_m = True
                for part in range(2):
                    for i in range(4 * half, 4 * half + 4):
                        sb = part * 8 + i
                        tlo = (i - 4 * half) * 128
                        P.add("vector", lambda e, sb=sb, tlo=tlo: e.tensor_mul(
                            etile(sb)[:, tlo:tlo + 128], etile(sb)[:, tlo:tlo + 128],
                            mtile(sb)[:, tlo:tlo + 128]),
                            waits=[("S", last_exp)] if first_m else (), inc="V")
                        first_m = False
                mask_V = P.c("V")
                if pend is not None:
                    p_avp, p_head, p_half, p_rd_S, p_rdb_free_V = pend
                    P.add("tensor", lambda e: e.matmul(
                        PSE[:, :], ONESF[:, :], s_rd[:, :], start=True, stop=True),
                        waits=[("S", p_rd_S)], inc="T")
                    bc_T = P.c("T")
                    cw_ = [("T", bc_T)]
                    if p_rdb_free_V:
                        cw_.append(("V", p_rdb_free_V))
                    P.add("scalar", lambda e: e.copy(RDB[:, :], PSE[:, :]),
                          waits=cw_, inc="S")
                    rdb_S = P.c("S")
                    dst = A_LN[:, p_head * T + p_half * 512:
                               p_head * T + (p_half + 1) * 512]
                    P.add("vector", lambda e, dst=dst, avp=p_avp: e.tensor_mul(
                        dst[0:64, :], avp[0][0:64, :], RDB[0:64, :]),
                        waits=[("S", rdb_S)], inc="V")
                    P.add("vector", lambda e, dst=dst, avp=p_avp: e.tensor_mul(
                        dst[64:128, :], avp[1][64:128, :], RDB[64:128, :]), inc="V")
                    attout_ready_V = P.c("V")
                    attout_prev2[p_head % 2] = attout_ready_V
                if ((half == 0 and head >= 14) or half == 1) and \
                        gat_state["i"] < len(gat_pieces):
                    gather_piece()
                for n, (sb, tlo) in enumerate(kbs):
                    w = [("V", mask_V)] if n == 0 else []
                    if n == 0 and attout_prev2[head % 2]:
                        w.append(("V", attout_prev2[head % 2]))
                    P.add("tensor", lambda e, sb=sb, tlo=tlo, n=n, avp=avp: e.matmul(
                        avp[0][0:65, tlo:512], VAUG[:, sb * 65:(sb + 1) * 65],
                        etile(sb)[:, tlo:512],
                        start=(n == 0), stop=(n == len(kbs) - 1)), waits=w, inc="T")
                for n, (sb, tlo) in enumerate(kbs):
                    P.add("tensor", lambda e, sb=sb, tlo=tlo, n=n, avp=avp: e.matmul(
                        avp[1][64:128, tlo:512],
                        A_H[:, VSM_OFF + sb * 128 + 64: VSM_OFF + sb * 128 + 128],
                        etile(sb)[:, tlo:512], start=(n == 0), stop=(n == len(kbs) - 1),
                        tile_position=(0, 64)), inc="T")
                prev_av_T = P.c("T")
                P.add("scalar", lambda e, avp=avp: e.activation(
                    STAT[:, 512:1024], avp[0][64:65, :], AF.Ln),
                    waits=[("T", prev_av_T)], inc="S")
                P.add("scalar", lambda e: e.activation(
                    s_rd[:, :], STAT[:, 512:1024], AF.Exp, scale=np.float32(-1.0)), inc="S")
                pend = (avp, head, half, P.c("S"), attout_ready_V)
        p_avp, p_head, p_half, p_rd_S, p_rdb_free_V = pend
        P.add("tensor", lambda e: e.matmul(
            PSE[:, :], ONESF[:, :], s_rd[:, :], start=True, stop=True),
            waits=[("S", p_rd_S)], inc="T")
        bc_T = P.c("T")
        P.add("scalar", lambda e: e.copy(RDB[:, :], PSE[:, :]),
              waits=[("T", bc_T), ("V", p_rdb_free_V)], inc="S")
        rdb_S = P.c("S")
        dst = A_LN[:, p_head * T + p_half * 512: p_head * T + (p_half + 1) * 512]
        P.add("vector", lambda e, dst=dst, avp=p_avp: e.tensor_mul(
            dst[0:64, :], avp[0][0:64, :], RDB[0:64, :]),
            waits=[("S", rdb_S)], inc="V")
        P.add("vector", lambda e, dst=dst, avp=p_avp: e.tensor_mul(
            dst[64:128, :], avp[1][64:128, :], RDB[64:128, :]), inc="V")
        attn_done_V = P.c("V")
        dbg_store("attout", A_LN[:, :], [("V", attn_done_V)])
        dbg_store("q15b", A_QKV[:, 15 * T:16 * T], [("V", attn_done_V)])
        dbg_store("vsm", A_H[:, VSM_OFF:VSM_OFF + 2048], [("V", attn_done_V)])
        dbg_store("vaug", VAUG[:, :], [("V", attn_done_V)])

        wload_G.clear(); wmm_T.clear(); evac_S.clear()
        assert gat_state["i"] == len(gat_pieces)
        for i in range(NK):
            if i == 2:
                P.add("gpsimd", lambda e: e.collective_compute(
                    "AllGather", mybir.AluOpType.bypass,
                    replica_groups=[list(range(8))],
                    ins=[cc_u_in.ap().opt()], outs=[cc_u_out.ap().opt()]),
                    waits=gat_state["uW"] + [("C1", 1)], inc="C2")
            pair = ((PSA, PSB) if i % 2 == 0 else (PSC, PSD))
            stream_wchunk(i, d_wo[i, :, :],
                          lambda k: A_LN[:, k * T:(k + 1) * T], NK,
                          A_QKV[:, i * T:(i + 1) * T], BO[:, i:i + 1], pair,
                          rhs_ready_V=attn_done_V, base_T=qkv_last_T,
                          par_sem=True)
        P.add("gpsimd", lambda e: e.collective_compute(
            "AllGather", mybir.AluOpType.bypass,
            replica_groups=[list(range(8))],
            ins=[cc_d_in.ap().opt()], outs=[cc_d_out.ap().opt()]),
            waits=gat_state["dW"] + [("C2", 1)], inc="C3")
        P.add("gpsimd", lambda e: e.collective_compute(
            "AllGather", mybir.AluOpType.bypass,
            replica_groups=[list(range(8))],
            ins=[cc_b_in.ap().opt()], outs=[cc_b_out.ap().opt()]),
            waits=[("C3", 1)], inc="C4")
        wout_done_S = P.c("S")
        wout_last_T = P.c("T")

        ln2_kh = {}
        res1_k = {}

        def _res1_cb(k):
            P.add("vector", lambda e, k=k: e.tensor_add(
                xres(k), xres(k), A_QKV[:, k * T:(k + 1) * T]), inc="V")
            res1_k[k] = P.c("V")

        ln2_V = ln_pipeline(lambda k: A_QKV[:, k * T:(k + 1) * T],
                            lambda k: A_QKV[:, k * T:(k + 1) * T],
                            NK, lambda k: [("S", wout_done_S)], per_kh=ln2_kh,
                            chunk_done_cb=_res1_cb)
        res1_V = P.c("V")
        dbg_store("ln2", A_QKV[:, 0:NK * T], [("V", ln2_V)])
        dbg_store("xres1", A_X[:, :], [("V", res1_V)])

        ln3_kh = {}
        ln3_V = ln_pipeline(xres, lns, NK, lambda k: [("V", res1_k[k])],
                            per_kh=ln3_kh)

        dbg_store("ln3", A_LN[:, :], [("V", ln3_V)])
        if dbg:
            P.add("sync", lambda e: e.dma_start(
                out=d_dbg["gw1"], in_=cc_g_out.ap()[1 * 128:2 * 128, 0:2048]),
                waits=[("C1", 1)], inc="L", amt=16)
            P.add("sync", lambda e: e.dma_start(
                out=d_dbg["gw9"], in_=cc_g_out.ap()[1 * 128:2 * 128, 2048:4096]),
                waits=[("C1", 1)], inc="L", amt=16)
            P.add("sync", lambda e: e.dma_start(
                out=d_dbg["gw0"], in_=cc_g_out.ap()[0:128, 0:2048]),
                waits=[("C1", 1)], inc="L", amt=16)
            P.add("sync", lambda e: e.dma_start(
                out=d_dbg["uw0"], in_=cc_u_out.ap()[0:128, 0:2048]),
                waits=[("C2", 1)], inc="L", amt=16)
        def wslot4(i):
            off = (i % 4) * 1152 + ((i // 4) % 2) * 128
            return A_WST[:, off:off + 1024]

        lut_pass_V = {}
        piece_mm_T = {}
        load_L = {}
        cast_S = {}
        gpc = [0]

        def wt_piece(src_ap, slot_i, cneed, width=1024):
            i = gpc[0]; gpc[0] += 1
            wslot = wslot4(slot_i)[:, 0:width]
            w = [(cneed, 1)]
            if slot_i >= 4 and (slot_i - 4) in piece_mm_T:
                w.append(("T", piece_mm_T[slot_i - 4]))
            elif slot_i < 4:
                w.append(("T", wout_last_T))
            P.add("sync", lambda e, d=wslot, s=src_ap: e.dma_start(out=d, in_=s),
                  waits=w, inc="L", amt=16)
            load_L[i] = P.c("L")
            return i, wslot

        slot_i = [0]
        hmult_V = {}
        for ch in range(NI):
            rank, rr = ch % 8, ch // 8
            gset = (PSA, PSB) if ch % 2 == 0 else (PSE, PSF)
            uset = (PSC, PSD) if ch % 2 == 0 else (PSG, PSH)
            loads = {}
            for gu in range(2):
                ccout = (cc_g_out, cc_u_out)[gu]
                cneed = "C4"
                for piece in range(2):
                    si = slot_i[0]; slot_i[0] += 1
                    gp, wslot = wt_piece(
                        ccout.ap()[rank * 128:(rank + 1) * 128,
                                   rr * 2048 + piece * 1024:
                                   rr * 2048 + (piece + 1) * 1024], si, cneed)
                    loads[(gu, piece)] = (gp, wslot, si)
                    if dbg and ch == 0 and gu == 0:
                        P.add("sync", lambda e, d=d_dbg[("ws0", "ws1")[piece]],
                              w2=wslot: e.dma_start(out=d, in_=w2),
                              waits=[("L", load_L[gp])], inc="L", amt=16)
            gate_T = 0
            for gu in range(2):
                pair = (gset, uset)[gu]
                for piece in range(2):
                    gp, wslot, si = loads[(gu, piece)]
                    w0 = [("L", load_L[gp])]
                    if piece == 0 and gu == 0 and (ch - 2) in hmult_V:
                        w0.append(("V", hmult_V[ch - 2]))
                    if piece == 0 and gu == 0 and ch == 1:
                        w0.append(("V", ln3_V))
                    first = True
                    for kc in range(8):
                        k = piece * 8 + kc
                        for hf in range(2):
                            w_ = list(w0) if first else []
                            if (k, hf) in ln3_kh:
                                w_.append(("V", ln3_kh[(k, hf)]))
                            P.add("tensor", lambda e, wslot=wslot, kc=kc, k=k, hf=hf,
                                  ps=pair[hf]: e.matmul(
                                ps[:, :], wslot[:, kc * 128:(kc + 1) * 128],
                                lns(k)[:, hf * 512:(hf + 1) * 512],
                                start=(k == 0), stop=(k == NK - 1)),
                                waits=w_, inc="T")
                            first = False
                    piece_mm_T[si] = P.c("T")
                if gu == 0:
                    gate_T = P.c("T")
            up_T = P.c("T")
            silu_S = {}
            for hf in range(2):
                w_s = [("T", gate_T)] if hf == 0 else []
                if hf == 0 and (ch - 1) in hmult_V:
                    w_s.append(("V", hmult_V[ch - 1]))
                P.add("scalar", lambda e, ps=gset[hf], hf=hf: e.activation(
                    SILU[:, hf * 512:(hf + 1) * 512], ps[:, :], AF.Silu),
                    waits=w_s, inc="S")
                silu_S[hf] = P.c("S")
            for hf in range(2):
                P.add("vector", lambda e, ch=ch, hf=hf, ps=uset[hf]: e.tensor_mul(
                    hch(ch)[:, hf * 512:(hf + 1) * 512],
                    SILU[:, hf * 512:(hf + 1) * 512], ps[:, :]),
                    waits=[("S", silu_S[hf]), ("T", up_T)] if hf == 0
                          else [("S", silu_S[hf])], inc="V")
            hmult_V[ch] = P.c("V")
        ffn1_V = P.c("V")
        dbg_store("h0", A_H[:, 0:T], [("V", ffn1_V)])
        dbg_store("h1", A_H[:, T:2 * T], [("V", ffn1_V)])
        dbg_store("hlo", A_H[:, 0:25600], [("V", ffn1_V)])
        dbg_store("hhi", A_QKV[:, 0:18432], [("V", ffn1_V)])
        if dbg:
            P.add("sync", lambda e: e.dma_start(
                out=d_dbg["dw3"], in_=cc_d_out.ap()[1 * 128:2 * 128, 5504:11008]),
                waits=[("C4", 1)], inc="L", amt=16)

        down_evac_S = {}
        ipieces = [(0, 8), (8, 8), (16, 8), (24, 8), (32, 8), (40, 3)]
        for ch in range(NK):
            rank, sl_ = ch // 2, ch % 2
            ps_pair = ((PSA, PSB) if ch % 2 == 0 else (PSC, PSD))
            first_all = True
            for pcidx, (i0, np_) in enumerate(ipieces):
                si = slot_i[0]; slot_i[0] += 1
                gp, wslot = wt_piece(
                    cc_d_out.ap()[rank * 128:(rank + 1) * 128,
                                  sl_ * 5504 + pcidx * 1024:
                                  sl_ * 5504 + pcidx * 1024 + np_ * 128],
                    si, "C4", width=np_ * 128)
                mm_w = [("L", load_L[gp]), ("V", ffn1_V)]
                if ch >= 2 and (ch - 2) in down_evac_S:
                    mm_w.append(("S", down_evac_S[ch - 2]))
                first = True
                for ic in range(np_):
                    i = i0 + ic
                    for hf in range(2):
                        ps = ps_pair[hf]
                        P.add("tensor", lambda e, wslot=wslot, ic=ic, i=i, hf=hf, ps=ps: e.matmul(
                            ps[:, :], wslot[:, ic * 128:(ic + 1) * 128],
                            hch(i)[:, hf * 512:(hf + 1) * 512],
                            start=(i == 0), stop=(i == NI - 1)),
                            waits=mm_w if (first and first_all) else
                                  (mm_w[:1] if first else ()),
                            inc="T")
                        first = False
                first_all = False
                piece_mm_T[si] = P.c("T")
            dn_T = P.c("T")
            for hf in range(2):
                ps = ps_pair[hf]
                P.add("scalar", lambda e, ch=ch, hf=hf, ps=ps: e.copy(
                    lns(ch)[:, hf * 512:(hf + 1) * 512], ps[:, :]),
                    waits=[("T", dn_T)], inc="S")
            down_evac_S[ch] = P.c("S")
        down_S = P.c("S")
        dbg_store("ffn", A_LN[:, :], [("S", down_S)])

        ln4_kh = {}
        dbg_l2 = P.c("L")
        st_L = {}

        def _store_cb(k):
            ob = OUTB[:, 0:1024]
            w = []
            if k >= 1:
                w.append(("L", st_L[k - 1]))
            P.add("vector", lambda e, k=k, ob=ob: e.tensor_add(
                ob, xres(k), lns(k)), waits=w, inc="V")
            add_V = P.c("V")
            P.add("sync", lambda e, k=k, ob=ob: e.dma_start(
                out=d_out[:, k * T:(k + 1) * T], in_=ob),
                waits=[("V", add_V)], inc="L", amt=16)
            st_L[k] = P.c("L")

        ln4_V = ln_pipeline(lns, lns, NK,
                            lambda k: ([("S", down_S)] +
                                       ([("L", dbg_l2)] if dbg else [])),
                            per_kh=ln4_kh, chunk_done_cb=_store_cb)
        final_L = P.c("L")
        P.add("vector", lambda e: e.engine_nop(), waits=[("L", final_L)])

        def run_engine(name):
            def f(eng):
                for (en, waits, emit, incs) in P.steps:
                    if en != name:
                        continue
                    for (s, v) in waits:
                        eng.wait_ge(SEM[s], v)
                    ins = emit(eng)
                    for (s, a) in incs:
                        ins.then_inc(SEM[s], a)
            return f

        block.sync(run_engine("sync"))
        block.tensor(run_engine("tensor"))
        block.vector(run_engine("vector"))
        block.scalar(run_engine("scalar"))
        block.gpsimd(run_engine("gpsimd"))

    return nc


_NC_CACHE = None
_DBG = False
_LAST_RES = None


def _get_nc():
    global _NC_CACHE
    if _DBG:
        return build_nc(dbg=True)
    if _NC_CACHE is None:
        _NC_CACHE = build_nc()
    return _NC_CACHE


def _build_masks():
    tri = np.triu(np.ones((128, 128), dtype=np.float32))
    masks = []
    for p in range(2):
        M = np.zeros((16, 128, 8, 128), dtype=np.float32)
        for i in range(8):
            for j in range(8):
                if i < j:
                    M[i, :, j, :] = 1.0
                elif i == j:
                    M[i, :, j, :] = tri
                oth = 1.0 if (i < j or (p == 1 and i == j)) else 0.0
                M[8 + i, :, j, :] = oth
        masks.append(M.reshape(2048, 1024).astype(BF))
    return masks


def kernel(L0_attn_x, L0_attn_W_qkv, L0_attn_b_qkv, L0_attn_W_out, L0_attn_b_out,
           L0_ffn_lut, L0_ffn_gate_idx, L0_ffn_up_idx, L0_ffn_down_idx):
    x = np.ascontiguousarray(np.asarray(L0_attn_x, dtype=np.float32))
    Wq = np.asarray(L0_attn_W_qkv, dtype=np.float32)
    bq = np.asarray(L0_attn_b_qkv, dtype=np.float32)
    Wo = np.asarray(L0_attn_W_out, dtype=np.float32)
    bo = np.asarray(L0_attn_b_out, dtype=np.float32)
    lut = np.asarray(L0_ffn_lut, dtype=np.float32)
    gidx = np.asarray(L0_ffn_gate_idx, dtype=np.int32)
    uidx = np.asarray(L0_ffn_up_idx, dtype=np.int32)
    didx = np.asarray(L0_ffn_down_idx, dtype=np.int32)

    B, TT_, D_ = x.shape
    assert (B, TT_, D_) == (4, 2048, 2048)

    def wimg(W, nout):
        return np.ascontiguousarray(
            W.reshape(nout, 128, NK, 128).transpose(0, 3, 2, 1)
        ).reshape(nout, 128, NK * 128)

    wqL = wimg(Wq, NQKV)
    woL = wimg(Wo, NK)
    gL = np.ascontiguousarray(gidx.reshape(NI, 128, NK, 128).transpose(0, 3, 2, 1))
    uL = np.ascontiguousarray(uidx.reshape(NI, 128, NK, 128).transpose(0, 3, 2, 1))
    dL = np.ascontiguousarray(didx.reshape(NK, 128, NI, 128).transpose(0, 3, 2, 1))

    def slices_for(core, img, nslot, per):
        out = np.zeros((nslot, 128, img.shape[2] * img.shape[3]), dtype=np.int8)
        for r in range(nslot):
            ch = per * r + core if per == 8 else 2 * core + r
            if per == 8 and ch >= img.shape[0]:
                continue
            if per != 8:
                ch = 2 * core + r
            out[r] = img[ch].reshape(128, -1)
        return out
    bqL = np.ascontiguousarray(bq.reshape(NQKV, 128).T)
    boL = np.ascontiguousarray(bo.reshape(NK, 128).T)
    lutL = lut.reshape(1, 16)
    ident = np.eye(128, dtype=BF)
    ones_b = np.ones((128, 1), dtype=BF)
    ones_f = np.ones((1, 128), dtype=np.float32)
    ones_br = np.ones((1, 128), dtype=BF)
    masks = _build_masks()

    xb = x.reshape(B, 16, 128, D_)
    in_maps = []
    for c in range(8):
        b, p = c // 2, c % 2
        xown = xb[b, p::2]
        xoth = xb[b, 1 - p::2]
        def fm(xa):
            t = xa.reshape(1024, NK, 128)
            return np.ascontiguousarray(t.transpose(2, 1, 0))
        in_maps.append({
            "x_own": fm(xown.reshape(1024, 2048)).reshape(128, NK * T),
            "x_oth": fm(xoth.reshape(1024, 2048)).reshape(128, NK * T),
            "wq": wqL, "wo": woL, "bq": bqL, "bo": boL,
            "gslice": slices_for(c, gL, 6, 8),
            "uslice": slices_for(c, uL, 6, 8),
            "dslice": slices_for(c, dL, 2, 2),
            "lut": lutL,
            "mask": np.ascontiguousarray(
                masks[p].reshape(16, 128, 1024).transpose(1, 0, 2)).reshape(128, 16 * T),
            "ident": ident, "ones_b": ones_b, "ones_f": ones_f,
            "ones_br": ones_br,
        })

    nc = _get_nc()
    res = run_bass_kernel_spmd(nc, in_maps, core_ids=list(range(8)))
    global _LAST_RES
    _LAST_RES = res

    out = np.empty((B, 2048, D_), dtype=np.float32)
    for c in range(8):
        b, p = c // 2, c % 2
        o = res.results[c]["out"].reshape(128, NK, T)
        o = o.transpose(2, 1, 0).reshape(T, D_)
        out[b].reshape(16, 128, D_)[p::2] = o.reshape(8, 128, D_)
    return out


# version 42
# speedup vs baseline: 1.1058x; 1.1058x over previous
import numpy as np
import ml_dtypes

import concourse.bass as bass
import concourse.mybir as mybir
import concourse.bass_utils as _bass_utils_mod
from concourse.bass_utils import run_bass_kernel_spmd

if not getattr(_bass_utils_mod, "_ant_ldwopt_patched", False):
    _orig_run_command = _bass_utils_mod.run_command

    def _run_command_ldwopt(cmd, **kw):
        cmd = ["--enable-ldw-opt=true" if c == "--enable-ldw-opt=false" else c
               for c in cmd]
        return _orig_run_command(cmd, **kw)

    _bass_utils_mod.run_command = _run_command_ldwopt
    _bass_utils_mod._ant_ldwopt_patched = True


F32 = mybir.dt.float32
BF16 = mybir.dt.bfloat16
I32 = mybir.dt.int32
I8 = mybir.dt.int8
AF = mybir.ActivationFunctionType
BF = ml_dtypes.bfloat16

D = 2048
NK = 16
T = 1024
NQKV = 18
NH = 16
INTER = 5504
NI = 43
EXP_SHIFT = np.float32(-8.0)
INV_SQRT_HD = np.float32(1.0 / np.sqrt(128.0))



class Plan:

    def __init__(self):
        self.steps = []
        self.n = {}

    def c(self, sem):
        return self.n.get(sem, 0)

    def add(self, eng, emit, waits=(), inc=None, amt=1):
        for s, v in waits:
            assert v <= self.c(s), (s, v, self.c(s))
        incs = []
        if inc is not None:
            self.n[inc] = self.c(inc) + amt
            incs.append((inc, amt))
        self.steps.append((eng, tuple(waits), emit, tuple(incs)))
        return self.c(inc) if inc else None


def build_nc(dbg=False):
    nc = bass.Bass("TRN2", target_bir_lowering=False)

    d_x = nc.dram_tensor("x_own", [128, NK * T], F32, kind="ExternalInput").ap()
    d_xo = nc.dram_tensor("x_oth", [128, NK * T], F32, kind="ExternalInput").ap()
    d_wq = nc.dram_tensor("wq", [NQKV, 128, NK * 128], F32, kind="ExternalInput").ap()
    d_wo = nc.dram_tensor("wo", [NK, 128, NK * 128], F32, kind="ExternalInput").ap()
    d_bq = nc.dram_tensor("bq", [128, NQKV], F32, kind="ExternalInput").ap()
    d_bo = nc.dram_tensor("bo", [128, NK], F32, kind="ExternalInput").ap()
    d_gs = nc.dram_tensor("gslice", [6, 128, NK * 128], I8, kind="ExternalInput").ap()
    d_us = nc.dram_tensor("uslice", [6, 128, NK * 128], I8, kind="ExternalInput").ap()
    d_ds = nc.dram_tensor("dslice", [2, 128, NI * 128], I8, kind="ExternalInput").ap()
    cc_g_in = nc.dram_tensor("cc_g_in", [128, 6 * 2048], BF16)
    cc_u_in = nc.dram_tensor("cc_u_in", [128, 6 * 2048], BF16)
    cc_d_in = nc.dram_tensor("cc_d_in", [128, 2 * 5504], BF16)
    cc_g_out = nc.dram_tensor("cc_g_out", [8 * 128, 6 * 2048], BF16)
    cc_u_out = nc.dram_tensor("cc_u_out", [8 * 128, 6 * 2048], BF16)
    cc_d_out = nc.dram_tensor("cc_d_out", [8 * 128, 2 * 5504], BF16)
    cc_b_in = nc.dram_tensor("cc_b_in", [128, 16], BF16)
    cc_b_out = nc.dram_tensor("cc_b_out", [8 * 128, 16], BF16)
    d_lut = nc.dram_tensor("lut", [1, 16], F32, kind="ExternalInput").ap()
    d_mask = nc.dram_tensor("mask", [128, 16 * T], BF16, kind="ExternalInput").ap()
    d_ident = nc.dram_tensor("ident", [128, 128], BF16, kind="ExternalInput").ap()
    d_ones = nc.dram_tensor("ones_b", [128, 1], BF16, kind="ExternalInput").ap()
    d_onesf = nc.dram_tensor("ones_f", [1, 128], F32, kind="ExternalInput").ap()
    d_onesbr = nc.dram_tensor("ones_br", [1, 128], BF16, kind="ExternalInput").ap()
    d_out = nc.dram_tensor("out", [128, NK * T], F32, kind="ExternalOutput").ap()
    d_dbg = {}
    if dbg:
        for name, cols in [("ln1", NK * T), ("kv", 4096), ("q0", T),
                           ("attout", NK * T), ("ln2", NK * T), ("xres1", NK * T),
                           ("ln3", NK * T), ("h0", T), ("ffn", NK * T),
                           ("q15a", T), ("q15b", T), ("e15", 8192),
                           ("vsm", 2048), ("vaug", 1040),
                           ("gw1", 2048), ("gw9", 2048), ("dw3", 5504),
                           ("gw0", 2048), ("uw0", 2048), ("h1", T),
                           ("ws0", 1024), ("ws1", 1024),
                           ("hlo", 25600), ("hhi", 18432)]:
            d_dbg[name] = nc.dram_tensor("dbg_" + name, [128, cols], BF16,
                                         kind="ExternalOutput").ap()

    P = Plan()

    from contextlib import ExitStack
    with ExitStack() as _ctx:
        A_X = _ctx.enter_context(nc.sbuf_tensor("A_X", [128, NK * T], BF16))
        A_LN = _ctx.enter_context(nc.sbuf_tensor("A_LN", [128, NK * T], BF16))
        A_QKV = _ctx.enter_context(nc.sbuf_tensor("A_QKV", [128, 20480], BF16))
        A_H = _ctx.enter_context(nc.sbuf_tensor("A_H", [128, 25600], BF16))
        A_WST = _ctx.enter_context(nc.sbuf_tensor("A_WST", [128, 4608], BF16))
        A_I8 = _ctx.enter_context(nc.sbuf_tensor("A_I8", [128, 2048], I8))
        IDXB = _ctx.enter_context(nc.sbuf_tensor("IDXB", [128, 2048], BF16))
        A_GA = _ctx.enter_context(nc.sbuf_tensor("A_GA", [128, 1024], BF16))
        A_GW = _ctx.enter_context(nc.sbuf_tensor("A_GW", [128, 2048], BF16))
        SILU = _ctx.enter_context(nc.sbuf_tensor("SILU", [128, 1024], BF16))
        SQT = _ctx.enter_context(nc.sbuf_tensor("SQT", [128, 512], BF16))
        OUTB = _ctx.enter_context(nc.sbuf_tensor("OUTB", [128, 1024], F32))
        STAT = _ctx.enter_context(nc.sbuf_tensor("STAT", [1, 3072], F32))
        VAUG = _ctx.enter_context(nc.sbuf_tensor("VAUG", [128, 16 * 65], BF16))
        LUTB = _ctx.enter_context(nc.sbuf_tensor("LUTB", [128, 16], F32))
        IDENT = _ctx.enter_context(nc.sbuf_tensor("IDENT", [128, 128], BF16))
        ONESB = _ctx.enter_context(nc.sbuf_tensor("ONESB", [128, 1], BF16))
        ONESF = _ctx.enter_context(nc.sbuf_tensor("ONESF", [1, 128], F32))
        ONESBR = _ctx.enter_context(nc.sbuf_tensor("ONESBR", [1, 128], BF16))
        BQ = _ctx.enter_context(nc.sbuf_tensor("BQ", [128, NQKV], F32))
        BO = _ctx.enter_context(nc.sbuf_tensor("BO", [128, NK], F32))
        LUT1 = _ctx.enter_context(nc.sbuf_tensor("LUT1", [1, 16], F32))
        RDB = _ctx.enter_context(nc.sbuf_tensor("RDB", [128, 512], F32))
        PSA = _ctx.enter_context(nc.psum_tensor("PSA", [128, 512], F32))
        PSB = _ctx.enter_context(nc.psum_tensor("PSB", [128, 512], F32))
        PSC = _ctx.enter_context(nc.psum_tensor("PSC", [128, 512], F32))
        PSD = _ctx.enter_context(nc.psum_tensor("PSD", [128, 512], F32))
        PSE = _ctx.enter_context(nc.psum_tensor("PSE", [128, 512], F32))
        PSF = _ctx.enter_context(nc.psum_tensor("PSF", [128, 512], F32))
        PSG = _ctx.enter_context(nc.psum_tensor("PSG", [128, 512], F32))
        PSH = _ctx.enter_context(nc.psum_tensor("PSH", [128, 512], F32))
        sem_L = _ctx.enter_context(nc.semaphore("L"))
        sem_G = _ctx.enter_context(nc.semaphore("G"))
        sem_T = _ctx.enter_context(nc.semaphore("T"))
        sem_V = _ctx.enter_context(nc.semaphore("V"))
        sem_S = _ctx.enter_context(nc.semaphore("S"))
        sem_C1 = _ctx.enter_context(nc.semaphore("C1"))
        sem_C2 = _ctx.enter_context(nc.semaphore("C2"))
        sem_C3 = _ctx.enter_context(nc.semaphore("C3"))
        sem_C4 = _ctx.enter_context(nc.semaphore("C4"))
        sem_C5 = _ctx.enter_context(nc.semaphore("C5"))
        sem_G2 = _ctx.enter_context(nc.semaphore("G2"))
        sem_xtra = {}
        for _sn in ("GE", "GO", "UE", "UO", "DE", "DO", "LE", "LO",
                    "X1", "X2", "VT", "MK", "LA", "LB", "LC", "LD"):
            sem_xtra[_sn] = _ctx.enter_context(nc.semaphore(_sn))
        block = _ctx.enter_context(nc.Block())
        SEM = {"L": sem_L, "G": sem_G, "T": sem_T, "V": sem_V, "S": sem_S, "C1": sem_C1, "C2": sem_C2, "C3": sem_C3, "C4": sem_C4, "C5": sem_C5, "G2": sem_G2, **sem_xtra}

        def xres(k):
            return A_X[:, k * T:(k + 1) * T]

        def lns(k):
            return A_LN[:, k * T:(k + 1) * T]

        def qkv(ch):
            return A_QKV[:, ch * T:(ch + 1) * T]

        def kfm():
            return A_QKV[:, 16384:18432]

        def vfm():
            return A_QKV[:, 18432:20480]

        def hch(i):
            if i < 25:
                return A_H[:, i * T:(i + 1) * T]
            return A_QKV[:, (i - 25) * T:(i - 24) * T]

        E_OFF, MSK_OFF, VSM_OFF = 0, 8192, 16384
        def etile(sb):
            return A_H[:, E_OFF + sb * 512: E_OFF + (sb + 1) * 512]

        def mtile(sb):
            return A_H[:, MSK_OFF + sb * 512: MSK_OFF + (sb + 1) * 512]

        def vsm(sb):
            return A_H[:, VSM_OFF + sb * 128: VSM_OFF + (sb + 1) * 128]

        ln1o = A_H[:, 0:16384]
        xoth = A_QKV[:, 0:16384]

        s_mean = STAT[:, 0:1024]
        s_scr = STAT[:, 1024:2048]
        s_nmrs = STAT[:, 2048:3072]
        s_rstd = s_nmrs
        s_rd = STAT[:, 0:512]

        def dma(eng_name, dst, src, waits=(), sem="L"):
            P.add(eng_name, lambda e, d=dst, s=src: e.dma_start(out=d, in_=s),
                  waits=waits, inc=sem, amt=16)
            return P.c(sem)

        dma("sync", IDENT[:, :], d_ident)
        dma("sync", ONESB[:, :], d_ones)
        dma("sync", ONESF[:, :], d_onesf)
        dma("sync", ONESBR[:, :], d_onesbr)
        dma("sync", BQ[:, :], d_bq)
        dma("sync", BO[:, :], d_bo)
        dma("sync", LUT1[:, :], d_lut)
        consts_L = P.c("L")
        CW = ("L", consts_L)
        P.add("tensor", lambda e: e.matmul(
            PSA[:, 0:16], ONESF[:, :], LUT1[:, :], start=True, stop=True),
            waits=[("L", consts_L)], inc="T")
        P.add("scalar", lambda e: e.copy(LUTB[:, :], PSA[:, 0:16]),
              waits=[("T", P.c("T"))], inc="S")
        lutb_S = P.c("S")

        P.add("gpsimd", lambda e: e.dma_start(out=A_X[:, :], in_=d_x), inc="X1", amt=16)
        x_G = P.c("X1")

        def ln_pipeline(src_fn, dst_fn, nchunks, src_ready, per_kh=None,
                        chunk_done_cb=None):
            if per_kh is None:
                per_kh = {}
            """LayerNorm over feature-major chunks; stats via ones-matmuls.
            src_ready: list/fn of waits for chunk k being readable."""
            sq_S = [0] * nchunks
            for k in range(nchunks):
                w = list(src_ready(k))
                if k >= 1:
                    w.append(("T", P.c("T")))
                for half in range(2):
                    P.add("scalar", lambda e, k=k, half=half: e.activation(
                        SQT[:, :], src_fn(k)[:, half * 512:(half + 1) * 512],
                        AF.Square), waits=w if half == 0 else [("T", P.c("T"))],
                        inc="S")
                    sq_S[k] = P.c("S")
                    ps = (PSA, PSB)[half]
                    P.add("tensor", lambda e, k=k, half=half, ps=ps: e.matmul(
                        ps[0:1, :], ONESB[:, :],
                        src_fn(k)[:, half * 512:(half + 1) * 512],
                        start=(k == 0), stop=(k == nchunks - 1)),
                        waits=[("S", sq_S[k]), CW] if k == 0 and half == 0 else (),
                        inc="T")
                    ps2 = (PSC, PSD)[half]
                    P.add("tensor", lambda e, k=k, half=half, ps2=ps2: e.matmul(
                        ps2[0:1, :], ONESB[:, :], SQT[:, :],
                        start=(k == 0), stop=(k == nchunks - 1)),
                        waits=[("S", sq_S[k])], inc="T")
            mm_T = P.c("T")
            inv_n = 1.0 / 2048.0
            for half in range(2):
                psS = (PSA, PSB)[half]
                psQ = (PSC, PSD)[half]
                sl = slice(half * 512, (half + 1) * 512)
                P.add("vector", lambda e, psS=psS, sl=sl: e.tensor_copy(
                    s_mean[:, sl], psS[0:1, :]), waits=[("T", mm_T)], inc="V")
                P.add("vector", lambda e, psQ=psQ, sl=sl: e.tensor_copy(
                    s_scr[:, sl], psQ[0:1, :]), inc="V")
                P.add("vector", lambda e, sl=sl: e.scalar_tensor_tensor(
                    out=s_nmrs[:, sl], in0=s_mean[:, sl], scalar=inv_n,
                    in1=s_mean[:, sl], op0=mybir.AluOpType.mult,
                    op1=mybir.AluOpType.mult), inc="V")
                P.add("vector", lambda e, sl=sl: e.tensor_sub(
                    s_scr[:, sl], s_scr[:, sl], s_nmrs[:, sl]), inc="V")
                P.add("vector", lambda e, sl=sl: e.scalar_tensor_tensor(
                    out=s_mean[:, sl], in0=s_mean[:, sl], scalar=inv_n,
                    in1=s_mean[:, sl], op0=mybir.AluOpType.mult,
                    op1=mybir.AluOpType.bypass), inc="V")
            fix_V = P.c("V")
            P.add("scalar", lambda e: e.activation(
                s_rstd[:, :], s_scr[:, 0:1024], AF.Sqrt, bias=np.float32(1e-5),
                scale=np.float32(inv_n)), waits=[("V", fix_V)], inc="S")
            rstd_S = P.c("S")
            P.add("vector", lambda e: e.reciprocal(
                s_scr[:, 0:1024], s_rstd[:, :]), waits=[("S", rstd_S)], inc="V")
            P.add("vector", lambda e: e.scalar_tensor_tensor(
                out=s_nmrs[:, :], in0=s_mean[:, 0:1024], scalar=-1.0,
                in1=s_scr[:, 0:1024], op0=mybir.AluOpType.mult,
                op1=mybir.AluOpType.mult), inc="V")
            nmrs_V = P.c("V")
            bps = (PSE, PSF, PSG, PSH)
            srcs = (s_scr[:, 0:512], s_nmrs[:, 0:512],
                    s_scr[:, 512:1024], s_nmrs[:, 512:1024])
            for bi in range(4):
                P.add("tensor", lambda e, bi=bi: e.matmul(
                    bps[bi][:, :], ONESF[:, :], srcs[bi], start=True, stop=True),
                    waits=[("V", nmrs_V)] if bi == 0 else (), inc="T")
            bc_T = P.c("T")
            for k in range(nchunks):
                for half in range(2):
                    s2 = slice(half * 512, (half + 1) * 512)
                    pm = (PSE, PSG)[half]
                    pa = (PSF, PSH)[half]
                    P.add("vector", lambda e, k=k, s2=s2, pm=pm: e.tensor_mul(
                        dst_fn(k)[:, s2], src_fn(k)[:, s2], pm[:, :]),
                        waits=[("T", bc_T)] if k == 0 and half == 0 else (), inc="V")
                    P.add("vector", lambda e, k=k, s2=s2, pa=pa: e.tensor_add(
                        dst_fn(k)[:, s2], dst_fn(k)[:, s2], pa[:, :]), inc="V")
                    per_kh[(k, half)] = P.c("V")
                if chunk_done_cb is not None:
                    chunk_done_cb(k)
            return P.c("V")

        ln1_kh = {}
        ln1_V = ln_pipeline(xres, lns, NK, lambda k: [("X1", x_G)], per_kh=ln1_kh)
        def dbg_store(name, src_ap, waits):
            if dbg:
                P.add("sync", lambda e, s=src_ap, d=d_dbg[name]: e.dma_start(
                    out=d, in_=s), waits=waits, inc="L", amt=16)
        dbg_store("ln1", A_LN[:, :], [("V", ln1_V)])

        P.add("gpsimd", lambda e: e.dma_start(out=xoth, in_=d_xo), inc="X2", amt=16)
        xo_G = P.c("X2")
        ln1o_kh = {}
        ln1o_V = ln_pipeline(lambda k: xoth[:, k * T:(k + 1) * T],
                             lambda k: ln1o[:, k * T:(k + 1) * T],
                             NK, lambda k: [("X2", xo_G)], per_kh=ln1o_kh)

        gat_pieces = []
        for kind in ("g", "u"):
            for sl_ in range(6):
                for pc in range(2):
                    gat_pieces.append((kind, sl_, pc, 1024))
        for sl_ in range(2):
            for pc, w_ in enumerate([1024] * 5 + [384]):
                gat_pieces.append(("d", sl_, pc, w_))
        gat_state = {"i": 0, "nl": 0, "nr": 0, "cast_S": {}, "lut_V": {},
                     "store_L": {}, "load_L": {}}

        def issue_cc(inb, outb, wts, csem):
            P.add("gpsimd", lambda e, i=inb, o=outb: e.collective_compute(
                "AllGather", mybir.AluOpType.bypass,
                replica_groups=[list(range(8))],
                ins=[i.ap().opt()], outs=[o.ap().opt()]),
                waits=list(wts), inc=csem)

        def gp_load(gi):
            kind, sl_, pc, w_ = gat_pieces[gi]
            src = {"g": d_gs, "u": d_us, "d": d_ds}[kind]
            src_ap = src[sl_, :, :].rearrange("p (k o) -> p k o", o=128)[
                :, pc * 8:pc * 8 + w_ // 128, :]
            lw = [CW]
            if gi >= 2:
                lw.append(gat_state["cast_S"][gi - 2])
            b = A_I8[:, (gi % 2) * 1024:(gi % 2) * 1024 + w_]
            ls = ("LE", "LO")[gi % 2]
            P.add("sync", lambda e, b=b, s=src_ap: e.dma_start(
                out=b.rearrange("p (k o) -> p k o", o=128), in_=s),
                waits=lw, inc=ls, amt=16)
            gat_state["load_L"][gi] = (ls, P.c(ls))

        def gp_rest(gi):
            kind, sl_, pc, w_ = gat_pieces[gi]
            b = A_I8[:, (gi % 2) * 1024:(gi % 2) * 1024 + w_]
            ib = IDXB[:, (gi % 2) * 1024:(gi % 2) * 1024 + w_]
            cw = [gat_state["load_L"][gi]]
            if gi >= 2:
                cw.append(("V", gat_state["lut_V"][gi - 2]))
            P.add("vector", lambda e, ib=ib, b=b: e.tensor_copy(ib, b),
                  waits=cw, inc="V")
            gat_state["cast_S"][gi] = ("V", P.c("V"))
            gw = A_GW[:, (gi % 2) * 1024:(gi % 2) * 1024 + w_]
            pl = A_GA[:, 0:w_]
            vw = [gat_state["cast_S"][gi], ("S", lutb_S)]
            if gi >= 2:
                vw.append(gat_state["store_L"][gi - 2])
            P.add("vector", lambda e, ib=ib, gw=gw: e.tensor_scalar(
                gw, ib, 0.0, LUTB[:, 0:1],
                mybir.AluOpType.is_equal, mybir.AluOpType.mult),
                waits=vw, inc="V")
            for j in range(1, 16):
                P.add("vector", lambda e, ib=ib, pl=pl, j=j: e.tensor_scalar(
                    pl, ib, float(j), LUTB[:, j:j + 1],
                    mybir.AluOpType.is_equal, mybir.AluOpType.mult), inc="V")
                P.add("vector", lambda e, gw=gw, pl=pl: e.tensor_add(gw, gw, pl),
                      inc="V")
            gat_state["lut_V"][gi] = P.c("V")
            dstmap = {"g": (cc_g_in, 2048), "u": (cc_u_in, 2048),
                      "d": (cc_d_in, 5504)}
            dt_, wid = dstmap[kind]
            dst = dt_.ap()[:, sl_ * wid + pc * 1024: sl_ * wid + pc * 1024 + w_]
            ss = {"g": ("GE", "GO"), "u": ("UE", "UO"),
                  "d": ("DE", "DO")}[kind][gi % 2]
            P.add("sync", lambda e, dst=dst, gw=gw: e.dma_start(out=dst, in_=gw),
                  waits=[("V", gat_state["lut_V"][gi])], inc=ss, amt=16)
            gat_state["store_L"][gi] = (ss, P.c(ss))
            done = gi + 1
            if done == 12:
                gat_state["gW"] = [("GE", P.c("GE")), ("GO", P.c("GO"))]
            elif done == 24:
                gat_state["uW"] = [("UE", P.c("UE")), ("UO", P.c("UO"))]
            elif done == 36:
                gat_state["dW"] = [("DE", P.c("DE")), ("DO", P.c("DO"))]

        def gather_piece():
            if gat_state["nr"] >= len(gat_pieces):
                return
            while (gat_state["nl"] < len(gat_pieces)
                   and gat_state["nl"] <= gat_state["nr"] + 1):
                gp_load(gat_state["nl"])
                gat_state["nl"] += 1
            gp_rest(gat_state["nr"])
            gat_state["nr"] += 1
            gat_state["i"] = gat_state["nr"]

        def wslot_qkv(i):
            off = (i % 2) * 2304 + ((i // 2) % 2) * 128
            return A_WST[:, off:off + 2048]

        wload_G = {}
        wmm_T = {}
        evac_S = {}

        def stream_wchunk(i, src_ap, rhs_fn, nk, dst_evac, bias_ap, ps_pair,
                          extra_rhs=None, extra_ps=None, extra_dst=None,
                          rhs_ready_V=0, base_T=0, rhs_kh=None, par_sem=False):
            wbuf = wslot_qkv(i)
            w = []
            if i >= 2:
                w.append(("T", wmm_T[i - 2]))
            elif base_T:
                w.append(("T", base_T))
            gs = ("G2" if (par_sem and i % 2 == 1) else "G")
            P.add("gpsimd", lambda e, wbuf=wbuf, s=src_ap: e.dma_start(out=wbuf, in_=s),
                  waits=w, inc=gs, amt=16)
            wload_G[i] = (gs, P.c(gs))
            mm_waits = [wload_G[i]]
            if rhs_ready_V:
                mm_waits.append(("V", rhs_ready_V))
            if i >= 2 and (i - 2) in evac_S:
                mm_waits.append(("S", evac_S[i - 2]))
            elif i == 1 and extra_rhs is not None and 0 in evac_S:
                mm_waits.append(("S", evac_S[0]))
            first = True
            for k in range(nk):
                for half in range(2):
                    ps = ps_pair[half]
                    w_ = list(mm_waits) if first else []
                    if rhs_kh is not None and (k, half) in rhs_kh:
                        w_.append(("V", rhs_kh[(k, half)]))
                    P.add("tensor", lambda e, wbuf=wbuf, k=k, half=half, ps=ps: e.matmul(
                        ps[:, :], wbuf[:, k * 128:(k + 1) * 128],
                        rhs_fn(k)[:, half * 512:(half + 1) * 512],
                        start=(k == 0), stop=(k == nk - 1)),
                        waits=w_, inc="T")
                    first = False
                    if extra_rhs is not None:
                        ps2 = extra_ps[half]
                        P.add("tensor", lambda e, wbuf=wbuf, k=k, half=half, ps2=ps2: e.matmul(
                            ps2[:, :], wbuf[:, k * 128:(k + 1) * 128],
                            extra_rhs(k)[:, half * 512:(half + 1) * 512],
                            start=(k == 0), stop=(k == nk - 1)), inc="T")
            wmm_T[i] = P.c("T")
            for half in range(2):
                ps = ps_pair[half]
                P.add("scalar", lambda e, ps=ps, half=half: e.activation(
                    dst_evac[:, half * 512:(half + 1) * 512], ps[:, :],
                    AF.Identity, bias=bias_ap, scale=np.float32(1.0)),
                    waits=[("T", wmm_T[i]), CW], inc="S")
                if extra_rhs is not None:
                    ps2 = extra_ps[half]
                    P.add("scalar", lambda e, ps2=ps2, half=half: e.activation(
                        extra_dst[:, half * 512:(half + 1) * 512], ps2[:, :],
                        AF.Identity, bias=bias_ap, scale=np.float32(1.0)), inc="S")
            evac_S[i] = P.c("S")
            return evac_S[i]

        order = [16, 17] + list(range(16))
        for i, ch in enumerate(order):
            pair = ((PSA, PSB) if i % 2 == 0 else (PSC, PSD))
            if ch == 16:
                stream_wchunk(i, d_wq[ch, :, :], lns, NK, kfm()[:, 0:1024],
                              BQ[:, ch:ch + 1], pair,
                              extra_rhs=lambda k: ln1o[:, k * T:(k + 1) * T],
                              extra_ps=(PSE, PSF), extra_dst=kfm()[:, 1024:2048],
                              rhs_ready_V=max(ln1_V, ln1o_V), par_sem=True)
            elif ch == 17:
                stream_wchunk(i, d_wq[ch, :, :], lns, NK, vfm()[:, 0:1024],
                              BQ[:, ch:ch + 1], pair,
                              extra_rhs=lambda k: ln1o[:, k * T:(k + 1) * T],
                              extra_ps=(PSE, PSF), extra_dst=vfm()[:, 1024:2048],
                              rhs_ready_V=max(ln1_V, ln1o_V), par_sem=True)
            else:
                stream_wchunk(i, d_wq[ch, :, :], lns, NK, qkv(ch),
                              BQ[:, ch:ch + 1], pair, rhs_kh=ln1_kh, par_sem=True)
            gather_piece()
            if i == 1:
                kv_evac_S = evac_S[1]
                kv_mm_T = wmm_T[1]
                for sb in range(16):
                    P.add("sync", lambda e, sb=sb: e.dma_start_transpose(
                        A_H[:, VSM_OFF + sb * 128: VSM_OFF + (sb + 1) * 128],
                        vfm()[:, sb * 128:(sb + 1) * 128]),
                        waits=[("S", kv_evac_S)] if sb == 0 else (), inc="VT", amt=16)
                vt_L = P.c("VT")
                mask0_L = dma("sync", A_H[:, MSK_OFF:MSK_OFF + 8192].rearrange(
                    "p (s t) -> p s t", t=512),
                    d_mask.rearrange("p (s t) -> p s t", t=T)[:, :, 0:512],
                    waits=[("T", kv_mm_T)], sem="MK")
                P.add("vector", lambda e: e.memset(VAUG[:, :], 1.0), inc="V")
                P.add("vector", lambda e: e.tensor_copy(
                    VAUG[:, :].rearrange("p (s c) -> p s c", c=65)[:, :, 0:64],
                    A_H[:, VSM_OFF:VSM_OFF + 2048].rearrange(
                        "p (s c) -> p s c", c=128)[:, :, 0:64]),
                    waits=[("VT", vt_L)], inc="V")
                vaug_V = P.c("V")
        qkv_done_S = P.c("S")
        qkv_last_T = P.c("T")
        issue_cc(cc_g_in, cc_g_out, gat_state["gW"], "C1")
        dbg_store("kv", A_QKV[:, 16384:20480], [("S", qkv_done_S)])
        dbg_store("q0", A_QKV[:, 0:T], [("S", qkv_done_S)])
        dbg_store("q15a", A_QKV[:, 15 * T:16 * T], [("S", qkv_done_S)])

        attout_ready_V = 0
        prev_av_T = 0
        exp_S_hist = []
        sc_rot = [0]
        attout_prev2 = {0: 0, 1: 0}
        pend = None
        for half in range(2):
            if half == 0:
                mask_L = mask0_L
            else:
                mask_L = dma("sync", A_H[:, MSK_OFF:MSK_OFF + 8192].rearrange(
                    "p (s t) -> p s t", t=512),
                    d_mask.rearrange("p (s t) -> p s t", t=T)[:, :, 512:1024],
                    waits=[("T", prev_av_T)], sem="MK")
            for head in range(NH):
                q_ap = A_QKV[:, head * T + half * 512: head * T + (half + 1) * 512]
                kbs = []
                for part in range(2):
                    for i in range(8):
                        tlo = max(i - 4 * half, 0) * 128
                        if tlo < 512:
                            kbs.append((part * 8 + i, tlo))
                kbs.sort(key=lambda x: x[1])
                avp = (PSC, PSD) if head % 2 == 0 else (PSF, PSG)
                for n, (sb, tlo) in enumerate(kbs):
                    ps = (PSA, PSB, PSH)[sc_rot[0] % 3]
                    w = []
                    if head == 0 and n == 0:
                        w = [("MK", mask_L), ("S", qkv_done_S), ("V", vaug_V)]
                    if len(exp_S_hist) >= 3:
                        w.append(("S", exp_S_hist[-3]))
                    P.add("tensor", lambda e, sb=sb, tlo=tlo, ps=ps, q_ap=q_ap: e.matmul(
                        ps[:, tlo:512], kfm()[:, sb * 128:(sb + 1) * 128],
                        q_ap[:, tlo:512], start=True, stop=True), waits=w, inc="T")
                    sc_T = P.c("T")
                    P.add("scalar", lambda e, sb=sb, tlo=tlo, ps=ps: e.activation(
                        etile(sb)[:, tlo:512], ps[:, tlo:512], AF.Exp,
                        bias=EXP_SHIFT, scale=INV_SQRT_HD),
                        waits=[("T", sc_T)], inc="S")
                    exp_S_hist.append(P.c("S"))
                    sc_rot[0] += 1
                last_exp = exp_S_hist[-1]
                first_m = True
                for part in range(2):
                    for i in range(4 * half, 4 * half + 4):
                        sb = part * 8 + i
                        tlo = (i - 4 * half) * 128
                        P.add("vector", lambda e, sb=sb, tlo=tlo: e.tensor_mul(
                            etile(sb)[:, tlo:tlo + 128], etile(sb)[:, tlo:tlo + 128],
                            mtile(sb)[:, tlo:tlo + 128]),
                            waits=[("S", last_exp)] if first_m else (), inc="V")
                        first_m = False
                mask_V = P.c("V")
                if pend is not None:
                    p_avp, p_head, p_half, p_rd_S, p_rdb_free_V = pend
                    P.add("tensor", lambda e: e.matmul(
                        PSE[:, :], ONESF[:, :], s_rd[:, :], start=True, stop=True),
                        waits=[("S", p_rd_S)], inc="T")
                    bc_T = P.c("T")
                    cw_ = [("T", bc_T)]
                    if p_rdb_free_V:
                        cw_.append(("V", p_rdb_free_V))
                    P.add("scalar", lambda e: e.copy(RDB[:, :], PSE[:, :]),
                          waits=cw_, inc="S")
                    rdb_S = P.c("S")
                    dst = A_LN[:, p_head * T + p_half * 512:
                               p_head * T + (p_half + 1) * 512]
                    P.add("vector", lambda e, dst=dst, avp=p_avp: e.tensor_mul(
                        dst[0:64, :], avp[0][0:64, :], RDB[0:64, :]),
                        waits=[("S", rdb_S)], inc="V")
                    P.add("vector", lambda e, dst=dst, avp=p_avp: e.tensor_mul(
                        dst[64:128, :], avp[1][64:128, :], RDB[64:128, :]), inc="V")
                    attout_ready_V = P.c("V")
                    attout_prev2[p_head % 2] = attout_ready_V
                if ((half == 0 and head >= 14) or half == 1) and \
                        gat_state["i"] < len(gat_pieces):
                    gather_piece()
                for n, (sb, tlo) in enumerate(kbs):
                    w = [("V", mask_V)] if n == 0 else []
                    if n == 0 and attout_prev2[head % 2]:
                        w.append(("V", attout_prev2[head % 2]))
                    P.add("tensor", lambda e, sb=sb, tlo=tlo, n=n, avp=avp: e.matmul(
                        avp[0][0:65, tlo:512], VAUG[:, sb * 65:(sb + 1) * 65],
                        etile(sb)[:, tlo:512],
                        start=(n == 0), stop=(n == len(kbs) - 1)), waits=w, inc="T")
                for n, (sb, tlo) in enumerate(kbs):
                    P.add("tensor", lambda e, sb=sb, tlo=tlo, n=n, avp=avp: e.matmul(
                        avp[1][64:128, tlo:512],
                        A_H[:, VSM_OFF + sb * 128 + 64: VSM_OFF + sb * 128 + 128],
                        etile(sb)[:, tlo:512], start=(n == 0), stop=(n == len(kbs) - 1),
                        tile_position=(0, 64)), inc="T")
                prev_av_T = P.c("T")
                P.add("scalar", lambda e, avp=avp: e.activation(
                    STAT[:, 512:1024], avp[0][64:65, :], AF.Ln),
                    waits=[("T", prev_av_T)], inc="S")
                P.add("scalar", lambda e: e.activation(
                    s_rd[:, :], STAT[:, 512:1024], AF.Exp, scale=np.float32(-1.0)), inc="S")
                pend = (avp, head, half, P.c("S"), attout_ready_V)
        p_avp, p_head, p_half, p_rd_S, p_rdb_free_V = pend
        P.add("tensor", lambda e: e.matmul(
            PSE[:, :], ONESF[:, :], s_rd[:, :], start=True, stop=True),
            waits=[("S", p_rd_S)], inc="T")
        bc_T = P.c("T")
        P.add("scalar", lambda e: e.copy(RDB[:, :], PSE[:, :]),
              waits=[("T", bc_T), ("V", p_rdb_free_V)], inc="S")
        rdb_S = P.c("S")
        dst = A_LN[:, p_head * T + p_half * 512: p_head * T + (p_half + 1) * 512]
        P.add("vector", lambda e, dst=dst, avp=p_avp: e.tensor_mul(
            dst[0:64, :], avp[0][0:64, :], RDB[0:64, :]),
            waits=[("S", rdb_S)], inc="V")
        P.add("vector", lambda e, dst=dst, avp=p_avp: e.tensor_mul(
            dst[64:128, :], avp[1][64:128, :], RDB[64:128, :]), inc="V")
        attn_done_V = P.c("V")
        dbg_store("attout", A_LN[:, :], [("V", attn_done_V)])
        dbg_store("q15b", A_QKV[:, 15 * T:16 * T], [("V", attn_done_V)])
        dbg_store("vsm", A_H[:, VSM_OFF:VSM_OFF + 2048], [("V", attn_done_V)])
        dbg_store("vaug", VAUG[:, :], [("V", attn_done_V)])

        wload_G.clear(); wmm_T.clear(); evac_S.clear()
        assert gat_state["i"] == len(gat_pieces)
        for i in range(NK):
            if i == 2:
                P.add("gpsimd", lambda e: e.collective_compute(
                    "AllGather", mybir.AluOpType.bypass,
                    replica_groups=[list(range(8))],
                    ins=[cc_u_in.ap().opt()], outs=[cc_u_out.ap().opt()]),
                    waits=gat_state["uW"] + [("C1", 1)], inc="C2")
            pair = ((PSA, PSB) if i % 2 == 0 else (PSC, PSD))
            stream_wchunk(i, d_wo[i, :, :],
                          lambda k: A_LN[:, k * T:(k + 1) * T], NK,
                          A_QKV[:, i * T:(i + 1) * T], BO[:, i:i + 1], pair,
                          rhs_ready_V=attn_done_V, base_T=qkv_last_T,
                          par_sem=True)
        P.add("gpsimd", lambda e: e.collective_compute(
            "AllGather", mybir.AluOpType.bypass,
            replica_groups=[list(range(8))],
            ins=[cc_d_in.ap().opt()], outs=[cc_d_out.ap().opt()]),
            waits=gat_state["dW"] + [("C2", 1)], inc="C3")
        P.add("gpsimd", lambda e: e.collective_compute(
            "AllGather", mybir.AluOpType.bypass,
            replica_groups=[list(range(8))],
            ins=[cc_b_in.ap().opt()], outs=[cc_b_out.ap().opt()]),
            waits=[("C3", 1)], inc="C4")
        wout_done_S = P.c("S")
        wout_last_T = P.c("T")

        ln2_kh = {}
        res1_k = {}

        def _res1_cb(k):
            P.add("vector", lambda e, k=k: e.tensor_add(
                xres(k), xres(k), A_QKV[:, k * T:(k + 1) * T]), inc="V")
            res1_k[k] = P.c("V")

        ln2_V = ln_pipeline(lambda k: A_QKV[:, k * T:(k + 1) * T],
                            lambda k: A_QKV[:, k * T:(k + 1) * T],
                            NK, lambda k: [("S", wout_done_S)], per_kh=ln2_kh,
                            chunk_done_cb=_res1_cb)
        res1_V = P.c("V")
        dbg_store("ln2", A_QKV[:, 0:NK * T], [("V", ln2_V)])
        dbg_store("xres1", A_X[:, :], [("V", res1_V)])

        ln3_kh = {}
        ln3_V = ln_pipeline(xres, lns, NK, lambda k: [("V", res1_k[k])],
                            per_kh=ln3_kh)

        dbg_store("ln3", A_LN[:, :], [("V", ln3_V)])
        if dbg:
            P.add("sync", lambda e: e.dma_start(
                out=d_dbg["gw1"], in_=cc_g_out.ap()[1 * 128:2 * 128, 0:2048]),
                waits=[("C1", 1)], inc="L", amt=16)
            P.add("sync", lambda e: e.dma_start(
                out=d_dbg["gw9"], in_=cc_g_out.ap()[1 * 128:2 * 128, 2048:4096]),
                waits=[("C1", 1)], inc="L", amt=16)
            P.add("sync", lambda e: e.dma_start(
                out=d_dbg["gw0"], in_=cc_g_out.ap()[0:128, 0:2048]),
                waits=[("C1", 1)], inc="L", amt=16)
            P.add("sync", lambda e: e.dma_start(
                out=d_dbg["uw0"], in_=cc_u_out.ap()[0:128, 0:2048]),
                waits=[("C2", 1)], inc="L", amt=16)
        def wslot4(i):
            off = (i % 4) * 1152 + ((i // 4) % 2) * 128
            return A_WST[:, off:off + 1024]

        lut_pass_V = {}
        piece_mm_T = {}
        load_L = {}
        cast_S = {}
        gpc = [0]

        def wt_piece(src_ap, slot_i, cneed, width=1024):
            i = gpc[0]; gpc[0] += 1
            wslot = wslot4(slot_i)[:, 0:width]
            w = [(cneed, 1)]
            if slot_i >= 4 and (slot_i - 4) in piece_mm_T:
                w.append(("T", piece_mm_T[slot_i - 4]))
            elif slot_i < 4:
                w.append(("T", wout_last_T))
            ls = ("LA", "LB", "LC", "LD")[slot_i % 4]
            P.add("sync", lambda e, d=wslot, s=src_ap: e.dma_start(out=d, in_=s),
                  waits=w, inc=ls, amt=16)
            load_L[i] = (ls, P.c(ls))
            return i, wslot

        slot_i = [0]
        hmult_V = {}
        for ch in range(NI):
            rank, rr = ch % 8, ch // 8
            gset = (PSA, PSB) if ch % 2 == 0 else (PSE, PSF)
            uset = (PSC, PSD) if ch % 2 == 0 else (PSG, PSH)
            loads = {}
            for gu in range(2):
                ccout = (cc_g_out, cc_u_out)[gu]
                cneed = "C4"
                for piece in range(2):
                    si = slot_i[0]; slot_i[0] += 1
                    gp, wslot = wt_piece(
                        ccout.ap()[rank * 128:(rank + 1) * 128,
                                   rr * 2048 + piece * 1024:
                                   rr * 2048 + (piece + 1) * 1024], si, cneed)
                    loads[(gu, piece)] = (gp, wslot, si)
                    if dbg and ch == 0 and gu == 0:
                        P.add("sync", lambda e, d=d_dbg[("ws0", "ws1")[piece]],
                              w2=wslot: e.dma_start(out=d, in_=w2),
                              waits=[load_L[gp]], inc="L", amt=16)
            gate_T = 0
            for gu in range(2):
                pair = (gset, uset)[gu]
                for piece in range(2):
                    gp, wslot, si = loads[(gu, piece)]
                    w0 = [load_L[gp]]
                    if piece == 0 and gu == 0 and (ch - 2) in hmult_V:
                        w0.append(("V", hmult_V[ch - 2]))
                    if piece == 0 and gu == 0 and ch == 1:
                        w0.append(("V", ln3_V))
                    first = True
                    for kc in range(8):
                        k = piece * 8 + kc
                        for hf in range(2):
                            w_ = list(w0) if first else []
                            if (k, hf) in ln3_kh:
                                w_.append(("V", ln3_kh[(k, hf)]))
                            P.add("tensor", lambda e, wslot=wslot, kc=kc, k=k, hf=hf,
                                  ps=pair[hf]: e.matmul(
                                ps[:, :], wslot[:, kc * 128:(kc + 1) * 128],
                                lns(k)[:, hf * 512:(hf + 1) * 512],
                                start=(k == 0), stop=(k == NK - 1)),
                                waits=w_, inc="T")
                            first = False
                    piece_mm_T[si] = P.c("T")
                if gu == 0:
                    gate_T = P.c("T")
            up_T = P.c("T")
            silu_S = {}
            for hf in range(2):
                w_s = [("T", gate_T)] if hf == 0 else []
                if hf == 0 and (ch - 1) in hmult_V:
                    w_s.append(("V", hmult_V[ch - 1]))
                P.add("scalar", lambda e, ps=gset[hf], hf=hf: e.activation(
                    SILU[:, hf * 512:(hf + 1) * 512], ps[:, :], AF.Silu),
                    waits=w_s, inc="S")
                silu_S[hf] = P.c("S")
            for hf in range(2):
                P.add("vector", lambda e, ch=ch, hf=hf, ps=uset[hf]: e.tensor_mul(
                    hch(ch)[:, hf * 512:(hf + 1) * 512],
                    SILU[:, hf * 512:(hf + 1) * 512], ps[:, :]),
                    waits=[("S", silu_S[hf]), ("T", up_T)] if hf == 0
                          else [("S", silu_S[hf])], inc="V")
            hmult_V[ch] = P.c("V")
        ffn1_V = P.c("V")
        dbg_store("h0", A_H[:, 0:T], [("V", ffn1_V)])
        dbg_store("h1", A_H[:, T:2 * T], [("V", ffn1_V)])
        dbg_store("hlo", A_H[:, 0:25600], [("V", ffn1_V)])
        dbg_store("hhi", A_QKV[:, 0:18432], [("V", ffn1_V)])
        if dbg:
            P.add("sync", lambda e: e.dma_start(
                out=d_dbg["dw3"], in_=cc_d_out.ap()[1 * 128:2 * 128, 5504:11008]),
                waits=[("C4", 1)], inc="L", amt=16)

        down_evac_S = {}
        ipieces = [(0, 8), (8, 8), (16, 8), (24, 8), (32, 8), (40, 3)]
        for ch in range(NK):
            rank, sl_ = ch // 2, ch % 2
            ps_pair = ((PSA, PSB) if ch % 2 == 0 else (PSC, PSD))
            first_all = True
            for pcidx, (i0, np_) in enumerate(ipieces):
                si = slot_i[0]; slot_i[0] += 1
                gp, wslot = wt_piece(
                    cc_d_out.ap()[rank * 128:(rank + 1) * 128,
                                  sl_ * 5504 + pcidx * 1024:
                                  sl_ * 5504 + pcidx * 1024 + np_ * 128],
                    si, "C4", width=np_ * 128)
                mm_w = [load_L[gp], ("V", ffn1_V)]
                if ch >= 2 and (ch - 2) in down_evac_S:
                    mm_w.append(("S", down_evac_S[ch - 2]))
                first = True
                for ic in range(np_):
                    i = i0 + ic
                    for hf in range(2):
                        ps = ps_pair[hf]
                        P.add("tensor", lambda e, wslot=wslot, ic=ic, i=i, hf=hf, ps=ps: e.matmul(
                            ps[:, :], wslot[:, ic * 128:(ic + 1) * 128],
                            hch(i)[:, hf * 512:(hf + 1) * 512],
                            start=(i == 0), stop=(i == NI - 1)),
                            waits=mm_w if (first and first_all) else
                                  (mm_w[:1] if first else ()),
                            inc="T")
                        first = False
                first_all = False
                piece_mm_T[si] = P.c("T")
            dn_T = P.c("T")
            for hf in range(2):
                ps = ps_pair[hf]
                P.add("scalar", lambda e, ch=ch, hf=hf, ps=ps: e.copy(
                    lns(ch)[:, hf * 512:(hf + 1) * 512], ps[:, :]),
                    waits=[("T", dn_T)], inc="S")
            down_evac_S[ch] = P.c("S")
        down_S = P.c("S")
        dbg_store("ffn", A_LN[:, :], [("S", down_S)])

        ln4_kh = {}
        dbg_l2 = P.c("L")
        st_L = {}

        def _store_cb(k):
            ob = OUTB[:, 0:1024]
            w = []
            if k >= 1:
                w.append(("L", st_L[k - 1]))
            P.add("vector", lambda e, k=k, ob=ob: e.tensor_add(
                ob, xres(k), lns(k)), waits=w, inc="V")
            add_V = P.c("V")
            P.add("sync", lambda e, k=k, ob=ob: e.dma_start(
                out=d_out[:, k * T:(k + 1) * T], in_=ob),
                waits=[("V", add_V)], inc="L", amt=16)
            st_L[k] = P.c("L")

        ln4_V = ln_pipeline(lns, lns, NK,
                            lambda k: ([("S", down_S)] +
                                       ([("L", dbg_l2)] if dbg else [])),
                            per_kh=ln4_kh, chunk_done_cb=_store_cb)
        final_L = P.c("L")
        P.add("vector", lambda e: e.engine_nop(), waits=[("L", final_L)])

        def run_engine(name):
            def f(eng):
                for (en, waits, emit, incs) in P.steps:
                    if en != name:
                        continue
                    for (s, v) in waits:
                        eng.wait_ge(SEM[s], v)
                    ins = emit(eng)
                    for (s, a) in incs:
                        ins.then_inc(SEM[s], a)
            return f

        block.sync(run_engine("sync"))
        block.tensor(run_engine("tensor"))
        block.vector(run_engine("vector"))
        block.scalar(run_engine("scalar"))
        block.gpsimd(run_engine("gpsimd"))

    return nc


_NC_CACHE = None
_DBG = False
_LAST_RES = None


def _get_nc():
    global _NC_CACHE
    if _DBG:
        return build_nc(dbg=True)
    if _NC_CACHE is None:
        _NC_CACHE = build_nc()
    return _NC_CACHE


def _build_masks():
    tri = np.triu(np.ones((128, 128), dtype=np.float32))
    masks = []
    for p in range(2):
        M = np.zeros((16, 128, 8, 128), dtype=np.float32)
        for i in range(8):
            for j in range(8):
                if i < j:
                    M[i, :, j, :] = 1.0
                elif i == j:
                    M[i, :, j, :] = tri
                oth = 1.0 if (i < j or (p == 1 and i == j)) else 0.0
                M[8 + i, :, j, :] = oth
        masks.append(M.reshape(2048, 1024).astype(BF))
    return masks


def kernel(L0_attn_x, L0_attn_W_qkv, L0_attn_b_qkv, L0_attn_W_out, L0_attn_b_out,
           L0_ffn_lut, L0_ffn_gate_idx, L0_ffn_up_idx, L0_ffn_down_idx):
    x = np.ascontiguousarray(np.asarray(L0_attn_x, dtype=np.float32))
    Wq = np.asarray(L0_attn_W_qkv, dtype=np.float32)
    bq = np.asarray(L0_attn_b_qkv, dtype=np.float32)
    Wo = np.asarray(L0_attn_W_out, dtype=np.float32)
    bo = np.asarray(L0_attn_b_out, dtype=np.float32)
    lut = np.asarray(L0_ffn_lut, dtype=np.float32)
    gidx = np.asarray(L0_ffn_gate_idx, dtype=np.int32)
    uidx = np.asarray(L0_ffn_up_idx, dtype=np.int32)
    didx = np.asarray(L0_ffn_down_idx, dtype=np.int32)

    B, TT_, D_ = x.shape
    assert (B, TT_, D_) == (4, 2048, 2048)

    def wimg(W, nout):
        return np.ascontiguousarray(
            W.reshape(nout, 128, NK, 128).transpose(0, 3, 2, 1)
        ).reshape(nout, 128, NK * 128)

    wqL = wimg(Wq, NQKV)
    woL = wimg(Wo, NK)
    gL = np.ascontiguousarray(gidx.reshape(NI, 128, NK, 128).transpose(0, 3, 2, 1))
    uL = np.ascontiguousarray(uidx.reshape(NI, 128, NK, 128).transpose(0, 3, 2, 1))
    dL = np.ascontiguousarray(didx.reshape(NK, 128, NI, 128).transpose(0, 3, 2, 1))

    def slices_for(core, img, nslot, per):
        out = np.zeros((nslot, 128, img.shape[2] * img.shape[3]), dtype=np.int8)
        for r in range(nslot):
            ch = per * r + core if per == 8 else 2 * core + r
            if per == 8 and ch >= img.shape[0]:
                continue
            if per != 8:
                ch = 2 * core + r
            out[r] = img[ch].reshape(128, -1)
        return out
    bqL = np.ascontiguousarray(bq.reshape(NQKV, 128).T)
    boL = np.ascontiguousarray(bo.reshape(NK, 128).T)
    lutL = lut.reshape(1, 16)
    ident = np.eye(128, dtype=BF)
    ones_b = np.ones((128, 1), dtype=BF)
    ones_f = np.ones((1, 128), dtype=np.float32)
    ones_br = np.ones((1, 128), dtype=BF)
    masks = _build_masks()

    xb = x.reshape(B, 16, 128, D_)
    in_maps = []
    for c in range(8):
        b, p = c // 2, c % 2
        xown = xb[b, p::2]
        xoth = xb[b, 1 - p::2]
        def fm(xa):
            t = xa.reshape(1024, NK, 128)
            return np.ascontiguousarray(t.transpose(2, 1, 0))
        in_maps.append({
            "x_own": fm(xown.reshape(1024, 2048)).reshape(128, NK * T),
            "x_oth": fm(xoth.reshape(1024, 2048)).reshape(128, NK * T),
            "wq": wqL, "wo": woL, "bq": bqL, "bo": boL,
            "gslice": slices_for(c, gL, 6, 8),
            "uslice": slices_for(c, uL, 6, 8),
            "dslice": slices_for(c, dL, 2, 2),
            "lut": lutL,
            "mask": np.ascontiguousarray(
                masks[p].reshape(16, 128, 1024).transpose(1, 0, 2)).reshape(128, 16 * T),
            "ident": ident, "ones_b": ones_b, "ones_f": ones_f,
            "ones_br": ones_br,
        })

    nc = _get_nc()
    res = run_bass_kernel_spmd(nc, in_maps, core_ids=list(range(8)))
    global _LAST_RES
    _LAST_RES = res

    out = np.empty((B, 2048, D_), dtype=np.float32)
    for c in range(8):
        b, p = c // 2, c % 2
        o = res.results[c]["out"].reshape(128, NK, T)
        o = o.transpose(2, 1, 0).reshape(T, D_)
        out[b].reshape(16, 128, D_)[p::2] = o.reshape(8, 128, D_)
    return out
